# version 31
# baseline (speedup 1.0000x reference)
import numpy as np
import concourse.bass as bass
import concourse.mybir as mybir
from concourse.bass_utils import run_bass_kernel_spmd
from contextlib import ExitStack

F32 = mybir.dt.float32
BF16 = mybir.dt.bfloat16
AF = mybir.ActivationFunctionType
ALU = mybir.AluOpType

D = 1024
SEQ = 8192
BATCH = 4
NCORES = 8
TILE = 128
TCORE = SEQ // 2
NT = TCORE // TILE
EPS_LN = 1e-5
EPS_FFT = 1e-8

ENGINES = ("sync", "gpsimd", "scalar", "vector", "tensor")


def _host_constants():
    NF = 513
    f = np.arange(NF)
    e = np.arange(D)
    ang = 2.0 * np.pi * np.outer(e, f) / D
    Mre = np.cos(ang)
    Mim = -np.sin(ang)
    mpack = np.concatenate([Mre[:, 1:513], Mim[:, 1:512], Mre[:, 0:1]], axis=1)
    d_ = np.arange(D)
    ang2 = 2.0 * np.pi * np.outer(f, d_) / D
    cf = np.ones(NF) * 2.0
    cf[0] = 1.0
    cf[512] = 1.0
    Cre = (cf[:, None] * np.cos(ang2)) / D
    Cim = (-2.0 * np.sin(ang2[1:512])) / D
    cpack = np.concatenate([Cre[1:513], Cim, Cre[0:1]], axis=0)
    tri = np.triu(np.ones((TILE, TILE)))
    iden = np.eye(TILE)
    return (
        np.ascontiguousarray(mpack, dtype=np.float32),
        np.ascontiguousarray(cpack, dtype=np.float32),
        np.ascontiguousarray(tri, dtype=np.float32),
        np.ascontiguousarray(iden, dtype=np.float32),
    )


class Sched:

    DMA_POOL = 6

    def __init__(self, start=None, bufs=None):
        self.q = {e: [] for e in ENGINES}
        self.count = dict(start) if start else {}
        self.bufs = dict(bufs) if bufs else {}

    def _bump(self, sem, inc):
        v = self.count.get(sem, 0) + inc
        self.count[sem] = v
        return v

    def step(self, eng, emit, reads=(), writes=(), dma=False, extra_waits=None):
        waits = {}
        if extra_waits:
            waits.update(extra_waits)

        def dep(sem, val):
            waits[sem] = max(waits.get(sem, 0), val)

        for b in reads:
            rec = self.bufs.get(b)
            if rec and rec[0]:
                dep(*rec[0])
        for b in writes:
            rec = self.bufs.get(b)
            if rec:
                if rec[0]:
                    dep(*rec[0])
                for r in rec[1]:
                    dep(*r)
        if dma:
            idx = self.count.get(f"{eng}_dmai", 0)
            self.count[f"{eng}_dmai"] = idx + 1
            sem = f"{eng}_d{idx % self.DMA_POOL}"
            cur = self.count.get(sem, 0)
            if cur > 0:
                dep(sem, cur)
            val = self._bump(sem, 16)
        else:
            sem = eng
            val = self._bump(sem, 1)
        self.q[eng].append((emit, dict(waits), sem, 16 if dma else 1))
        for b in reads:
            self.bufs.setdefault(b, [None, []])[1].append((sem, val))
        for b in writes:
            self.bufs[b] = [(sem, val), []]
        return (sem, val)

    def replay(self, block, sems, final_wait=True):
        waited = {e: {} for e in ENGINES}

        def body_for(name):
            def body(engobj):
                w = waited[name]
                for emit, waits, sem, inc in self.q[name]:
                    for f, v in waits.items():
                        if w.get(f, 0) < v:
                            engobj.wait_ge(sems[f], v)
                            w[f] = v
                    last = emit(engobj)
                    last.then_inc(sems[sem], inc)
                if final_wait and name in ("sync", "gpsimd"):
                    for j in range(Sched.DMA_POOL):
                        s = f"{name}_d{j}"
                        v = self.count.get(s, 0)
                        if v > 0 and waited[name].get(s, 0) < v:
                            engobj.wait_ge(sems[s], v)

            return body

        for name in ENGINES:
            if self.q[name]:
                getattr(block, name)(body_for(name))


def _load_act_set(s, nc, set_name):
    from concourse.hw_specs import get_activation_tables
    idx = list(get_activation_tables(nc.m.arch)).index(set_name)
    return s.add_instruction(mybir.InstLoadActFuncSet(
        name=nc.get_next_instruction_name(), act_func_set_id=idx, ins=[], outs=[]))


def build_program():
    nc = bass.Bass("TRN2", target_bir_lowering=False)

    x_d = nc.dram_tensor("x", [TCORE, D], F32, kind="ExternalInput").ap()
    xprev_d = nc.dram_tensor("xprev", [1, D], F32, kind="ExternalInput").ap()
    sel_d = nc.dram_tensor("sel", [8, 1], F32, kind="ExternalInput").ap()
    wq_d = nc.dram_tensor("wq", [D, D], F32, kind="ExternalInput").ap()
    wk_d = nc.dram_tensor("wk", [D, D], F32, kind="ExternalInput").ap()
    wv_d = nc.dram_tensor("wv", [D, D], F32, kind="ExternalInput").ap()
    wrt_d = nc.dram_tensor("wrt", [D, D], F32, kind="ExternalInput").ap()
    wk2t_d = nc.dram_tensor("wk2t", [D, D], F32, kind="ExternalInput").ap()
    wv2t_d = nc.dram_tensor("wv2t", [D, D], F32, kind="ExternalInput").ap()
    mur_d = nc.dram_tensor("mur", [TILE, 8], F32, kind="ExternalInput").ap()
    muk_d = nc.dram_tensor("muk", [TILE, 8], F32, kind="ExternalInput").ap()
    mpack_d = nc.dram_tensor("mpack", [D, D], F32, kind="ExternalInput").ap()
    cpack_d = nc.dram_tensor("cpack", [D, D], F32, kind="ExternalInput").ap()
    tri_d = nc.dram_tensor("tri", [TILE, TILE], F32, kind="ExternalInput").ap()
    iden_d = nc.dram_tensor("iden", [TILE, TILE], F32, kind="ExternalInput").ap()
    y_d = nc.dram_tensor("y", [TCORE, D], F32, kind="ExternalOutput").ap()
    sp_xnT_d = nc.dram_tensor("sp_xnT", [NT * TILE, 8 * 130], BF16).ap()
    sp_fkv_d = nc.dram_tensor("sp_fkv", [NT * TILE, D], BF16).ap()
    sp_fqn_d = nc.dram_tensor("sp_fqn", [NT * TILE, D], BF16).ap()
    sp_carry_d = nc.dram_tensor("sp_carry", [NT, D], F32).ap()
    cc_in = nc.dram_tensor("cc_in", [1, D], F32)
    cc_out = nc.dram_tensor("cc_out", [8, D], F32)

    es = ExitStack()

    def S(name, shape, dt):
        return es.enter_context(nc.sbuf_tensor(name, shape, dt))

    def Ps(name, shape, dt):
        return es.enter_context(nc.psum_tensor(name, shape, dt))

    A_sb = S("A_sb", [TILE, 8 * 3 * D], BF16)
    C_sb = S("C_sb", [TILE, 8 * D], BF16)
    WrT = S("WrT", [TILE, 8 * D], BF16)
    Wk2T = S("Wk2T", [TILE, 8 * D], BF16)
    Wv2T = S("Wv2T", [TILE, 8 * D], BF16)
    trilb = S("trilb", [TILE, TILE], BF16)
    iden = S("iden_sb", [TILE, TILE], F32)
    idenb = S("idenb", [TILE, TILE], BF16)
    mur = S("mur_sb", [TILE, 8], F32)
    muk = S("muk_sb", [TILE, 8], F32)
    MUr = S("MUr", [TILE, 8 * TILE], BF16)
    MUk = S("MUk", [TILE, 8 * TILE], BF16)
    onesb = S("onesb", [TILE, TILE], BF16)
    ones_col = S("ones_col", [TILE, 1], BF16)
    ones_row = S("ones_row", [1, TILE], BF16)
    epsLN = S("epsLN", [TILE, 1], F32)
    epsE = S("epsE", [TILE, 1], F32)
    zerof = S("zerof", [TILE, 1], F32)
    dumm = S("dumm", [1, 1], F32)
    carry_in_b = S("carry_in_b", [1, D], BF16)
    xbound = S("xbound", [TILE, 8], BF16)
    lnstats = S("lnstats", [TILE, 3 * NT], F32)
    sel_sb = S("sel_sb", [8, 1], F32)
    ag_sb = S("ag_sb", [8, D], F32)

    psA = Ps("psA", [TILE, D], F32)
    psB = Ps("psB", [TILE, D], F32)
    psT = Ps("psT", [TILE, D], BF16)
    psC = Ps("psC", [TILE, D], F32)

    sems = {e: es.enter_context(nc.semaphore(f"sem_{e}")) for e in ENGINES}
    for qe in ("sync", "gpsimd", "scalar"):
        for j in range(Sched.DMA_POOL):
            sems[f"{qe}_d{j}"] = es.enter_context(nc.semaphore(f"sem_{qe}_d{j}"))

    es_pro = ExitStack()
    Mb = es_pro.enter_context(nc.sbuf_tensor("Mb", [TILE, 8 * D], BF16))
    Wb = es_pro.enter_context(nc.sbuf_tensor("Wb", [TILE, 8 * D], BF16))
    stg = [es_pro.enter_context(nc.sbuf_tensor(f"stg{j}", [TILE, D], F32)) for j in range(2)]
    pvrow = es_pro.enter_context(nc.sbuf_tensor("pvrow", [1, D], F32))
    pvn = es_pro.enter_context(nc.sbuf_tensor("pvn", [1, D], F32))
    pv_ss = es_pro.enter_context(nc.sbuf_tensor("pv_ss", [1, 2], F32))
    pv_st = es_pro.enter_context(nc.sbuf_tensor("pv_st", [1, 6], F32))

    sch = Sched()
    st = sch.step

    def em_memsets(g):
        g.memset(onesb[:, :], 1.0)
        g.memset(ones_col[:, :], 1.0)
        g.memset(ones_row[:, :], 1.0)
        g.memset(epsLN[:, :], EPS_LN)
        g.memset(epsE[:, :], EPS_FFT)
        return g.memset(zerof[:, :], 0.0)

    st("gpsimd", em_memsets,
       writes=["onesb", "ones_col", "ones_row", "epsLN", "epsE", "zerof"])

    st("sync", (lambda s: s.dma_start(out=iden[:, :], in_=iden_d[:, :])), writes=["iden"], dma=True)
    st("sync", (lambda s: s.dma_start(out=mur[:, :], in_=mur_d[:, :])), writes=["mur"], dma=True)
    st("sync", (lambda s: s.dma_start(out=muk[:, :], in_=muk_d[:, :])), writes=["muk"], dma=True)
    st("sync", (lambda s: s.dma_start(out=pvrow[:, :], in_=xprev_d[:, :])), writes=["pvrow"], dma=True)
    st("sync", (lambda s: s.dma_start(out=sel_sb[:, :], in_=sel_d[:, :])), writes=["sel_sb"], dma=True)

    _lc = [0]

    def load_cast(dst, dst_off, srcap, rows, dsttok):
        j = _lc[0] % 2
        _lc[0] += 1
        st("sync", (lambda s, j=j: s.dma_start(out=stg[j][:, :],
                                               in_=srcap[rows:rows + TILE, :])),
           writes=[f"stg{j}"], dma=True)
        eng = "vector" if (_lc[0] % 2 == 0) else "scalar"
        if eng == "vector":
            st("vector", (lambda v, j=j: v.tensor_copy(dst[:, dst_off:dst_off + D], stg[j][:, :])),
               reads=[f"stg{j}"], writes=[dsttok])
        else:
            st("scalar", (lambda s, j=j: s.copy(out=dst[:, dst_off:dst_off + D], in_=stg[j][:, :])),
               reads=[f"stg{j}"], writes=[dsttok])

    st("sync", (lambda s: s.dma_start(out=stg[0][:, 0:TILE], in_=tri_d[:, :])),
       writes=["stg0"], dma=True)
    st("vector", (lambda v: v.tensor_copy(trilb[:, :], stg[0][:, 0:TILE])),
       reads=["stg0"], writes=["trilb"])
    for k in range(8):
        load_cast(Mb, k * D, mpack_d, k * TILE, f"Mb{k}")
    st("scalar", (lambda s: s.copy(out=idenb[:, :], in_=iden[:, :])),
       reads=["iden"], writes=["idenb"])

    for nm, dst, src, srctok in (("MUr", MUr, mur, "mur"), ("MUk", MUk, muk, "muk")):
        for k in range(8):
            st("scalar",
               (lambda s, dst=dst, src=src, k=k: s.activation(
                   out=dst[:, k * TILE:(k + 1) * TILE], in_=onesb[:, :],
                   func=AF.Identity, bias=zerof[:, :], scale=src[:, k:k + 1])),
               reads=["onesb", srctok, "zerof"], writes=[nm])

    for wi, (wsrc, acol) in enumerate(((wq_d, 0), (wk_d, D), (wv_d, 2 * D))):
        for k in range(8):
            load_cast(Wb, k * D, wsrc, k * TILE, f"Wb{k}")
        for m in range(8):
            ps = psA if (m % 2 == 0) else psB
            pstok = "psA" if (m % 2 == 0) else "psB"
            for n in range(2):
                def em_prep(t, m=m, n=n, ps=ps):
                    last = None
                    for kc in range(8):
                        last = t.matmul(
                            ps[:, n * 512:(n + 1) * 512],
                            Wb[:, kc * D + m * TILE:kc * D + (m + 1) * TILE],
                            Mb[:, kc * D + n * 512:kc * D + (n + 1) * 512],
                            start=(kc == 0), stop=(kc == 7))
                    return last

                st("tensor", em_prep,
                   reads=[f"Wb{k}" for k in range(8)] + [f"Mb{k}" for k in range(8)],
                   writes=[f"{pstok}_h{n}"])
                col = m * 3 * D + acol + n * 512
                if n == 0:
                    st("vector",
                       (lambda v, ps=ps, col=col: v.tensor_copy(
                           A_sb[:, col:col + 512], ps[:, 0:512])),
                       reads=[f"{pstok}_h0"], writes=["A_sb"])
                else:
                    st("scalar",
                       (lambda s, ps=ps, col=col: s.copy(
                           out=A_sb[:, col:col + 512], in_=ps[:, 512:1024])),
                       reads=[f"{pstok}_h1"], writes=["A_sb"])

    for k in range(8):
        load_cast(C_sb, k * D, cpack_d, k * TILE, "C_sb")
    for name, dst, wsrc in (("WrT", WrT, wrt_d), ("Wk2T", Wk2T, wk2t_d), ("Wv2T", Wv2T, wv2t_d)):
        for k in range(8):
            load_cast(dst, k * D, wsrc, k * TILE, name)
    st("scalar", (lambda s: (
        s.activation(out=pvn[:, :], in_=pvrow[:, :], func=AF.Square, accum_out=pv_ss[:, 0:1]),
        s.activation(out=pvn[:, :], in_=pvrow[:, :], func=AF.Copy, accum_out=pv_ss[:, 1:2]),
        s.copy(out=dumm[:, :], in_=zerof[0:1, :]))[-1]),
       reads=["pvrow"], writes=["pv_ss", "pvn"])
    st("vector", (lambda v: v.tensor_scalar(pv_st[:, 0:1], pv_ss[:, 1:2], 1.0 / D, None, op0=ALU.mult)),
       reads=["pv_ss"], writes=["pv_m"])
    st("vector", (lambda v: v.tensor_scalar(pv_st[:, 1:2], pv_ss[:, 0:1], 1.0 / D, None, op0=ALU.mult)),
       reads=["pv_ss"], writes=["pv_x2"])
    st("vector", (lambda v: v.tensor_tensor(pv_st[:, 2:3], pv_st[:, 0:1], pv_st[:, 0:1], op=ALU.mult)),
       reads=["pv_m"], writes=["pv_m2"])
    st("vector", (lambda v: v.tensor_tensor(pv_st[:, 1:2], pv_st[:, 1:2], pv_st[:, 2:3], op=ALU.subtract)),
       reads=["pv_x2", "pv_m2"], writes=["pv_var"])
    st("scalar", (lambda s: s.activation(out=pv_st[:, 3:4], in_=pv_st[:, 1:2], func=AF.Sqrt,
                                         bias=epsLN[0:1, :])),
       reads=["pv_var", "epsLN"], writes=["pv_sd"])
    st("vector", (lambda v: v.reciprocal(pv_st[:, 3:4], pv_st[:, 3:4])),
       reads=["pv_sd"], writes=["pv_rs"])
    st("vector", (lambda v: v.tensor_scalar(pv_st[:, 4:5], pv_st[:, 0:1], pv_st[:, 3:4], -1.0,
                                            op0=ALU.mult, op1=ALU.mult)),
       reads=["pv_m", "pv_rs"], writes=["pv_nb"])
    st("scalar", (lambda s: s.activation(out=pvn[:, :], in_=pvrow[:, :], func=AF.Identity,
                                         bias=pv_st[0:1, 4:5], scale=pv_st[0:1, 3:4])),
       reads=["pvrow", "pv_nb", "pv_rs", "pvn"], writes=["pvn"])
    st("tensor", (lambda t: [t.transpose(psA[:, k:k + 1], pvn[0:1, k * TILE:(k + 1) * TILE],
                                         iden[0:1, 0:1]) for k in range(8)][-1]),
       reads=["pvn", "iden", "psA_h0", "psA_h1"], writes=["psA_h0"])
    st("vector", (lambda v: v.tensor_copy(xbound[:, :], psA[:, 0:8])),
       reads=["psA_h0"], writes=["xbound"])

    with nc.Block() as blk1:
        sch.replay(blk1, sems, final_wait=False)
    es_pro.close()
    _carry_bufs = sch.bufs

    xt = [S(f"xt{p}", [TILE, D], F32) for p in range(2)]
    ssum = S("ssum", [TILE, 1], F32)
    smean = S("smean", [TILE, 1], F32)
    mean = S("mean", [TILE, 1], F32)
    msq = S("msq", [TILE, 1], F32)
    var = S("var", [TILE, 1], F32)
    sd = S("sd", [TILE, 1], F32)
    xnb = S("xnb", [TILE, D], BF16)
    sq_scr = S("sq_scr", [TILE, D], BF16)
    g_s = S("g_s", [TILE, D], F32)
    xnT = [S(f"xnT{p}", [TILE, 8 * 130], BF16) for p in range(2)]
    mg = S("mg", [TILE, 516], F32)
    mg2 = S("mg2", [TILE, 512], F32)
    inv_q = S("inv_q", [TILE, 516], F32)
    inv_k = S("inv_k", [TILE, 516], F32)
    fqn = S("fqn", [TILE, D], BF16)
    fkn = S("fkn", [TILE, D], BF16)
    fv = S("fv", [TILE, D], BF16)
    Pb_ = [S(f"Pb{p}", [TILE, D], BF16) for p in range(2)]
    fkvb = S("fkvb", [TILE, D], BF16)
    fkvb2 = S("fkvb2", [TILE, D], BF16)
    fqn2 = S("fqn2", [TILE, D], BF16)
    fqraw = S("fqraw", [TILE, D], BF16)
    fkraw = S("fkraw", [TILE, D], BF16)
    Zb = S("Zb", [TILE, D], BF16)
    ZT = S("ZT", [TILE, D], BF16)
    dm = S("dm", [TILE, 8 * TILE], BF16)
    b1 = S("b1", [TILE, 8 * TILE], BF16)
    b2 = S("b2", [TILE, 8 * TILE], BF16)
    xr = S("xr", [TILE, 8 * TILE], BF16)
    xk = S("xk", [TILE, 8 * TILE], BF16)
    r_sb = S("r_sb", [TILE, D], F32)
    kk = S("kk", [TILE, D], BF16)
    y1 = S("y1", [TILE, D], F32)
    y_sb = S("y_sb", [TILE, D], F32)
    snap_b = [S(f"snap_b{p}", [1, D], BF16) for p in range(2)]

    sch = Sched(sch.count, _carry_bufs)
    st = sch.step

    def xnT3(buf, lo, hi):
        return buf[:, :].rearrange("p (c k) -> p c k", k=130)[:, :, lo:hi]

    def em_proj(t, XN_T, acol, ps):
        last = None
        for n in range(2):
            for kc in range(8):
                last = t.matmul(
                    ps[:, n * 512:(n + 1) * 512],
                    XN_T[:, kc * 130 + 2:kc * 130 + 130],
                    A_sb[:, kc * 3 * D + acol + n * 512:kc * 3 * D + acol + (n + 1) * 512],
                    start=(kc == 0), stop=(kc == 7))
        return last

    def em_ffn_mm(t, inb, wsb, ps):
        last = None
        for n in range(2):
            for kc in range(8):
                last = t.matmul(
                    ps[:, n * 512:(n + 1) * 512],
                    inb[:, kc * TILE:(kc + 1) * TILE],
                    wsb[:, kc * D + n * 512:kc * D + (n + 1) * 512],
                    start=(kc == 0), stop=(kc == 7))
        return last

    def a_stage1(i):
        p = i % 2
        XT, XN_T = xt[p], xnT[p]
        xtn, xnTn = f"xt{p}", f"xnT{p}"
        st("sync", (lambda s, i=i, XT=XT: s.dma_start(
            out=XT[:, :], in_=x_d[i * TILE:(i + 1) * TILE, :])),
           writes=[xtn], dma=True)
        RST = lnstats[:, 3 * i:3 * i + 1]
        NBI = lnstats[:, 3 * i + 1:3 * i + 2]
        RP1 = lnstats[:, 3 * i + 2:3 * i + 3]
        st("scalar", (lambda s, XT=XT: (
            s.activation(out=sq_scr[:, :], in_=XT[:, :], func=AF.Square, accum_out=ssum[:, :]),
            s.activation(out=sq_scr[:, :], in_=XT[:, :], func=AF.Copy, accum_out=smean[:, :]),
            s.copy(out=dumm[:, :], in_=zerof[0:1, :]))[-1]),
           reads=[xtn], writes=["ssum", "smean", "sq_scr"])
        st("vector", (lambda v: v.tensor_scalar(msq[:, :], smean[:, :], smean[:, :], 1.0 / (D * D),
                                                op0=ALU.mult, op1=ALU.mult)),
           reads=["smean"], writes=["msq"])
        st("vector", (lambda v: v.scalar_tensor_tensor(var[:, :], ssum[:, :], 1.0 / D, msq[:, :],
                                                       op0=ALU.mult, op1=ALU.subtract)),
           reads=["ssum", "msq"], writes=["var"])
        st("scalar", (lambda s: s.activation(out=sd[:, :], in_=var[:, :], func=AF.Ln, bias=epsLN[:, :])),
           reads=["var", "epsLN"], writes=["sd"])
        st("scalar", (lambda s: s.activation(out=sd[:, :], in_=sd[:, :], func=AF.Exp, scale=0.5)),
           reads=["sd"], writes=["sd"])
        st("vector", (lambda v, RST=RST: v.reciprocal(RST, sd[:, :])),
           reads=["sd"], writes=[f"rstd{i}"])
        st("vector", (lambda v, RST=RST, NBI=NBI: v.tensor_scalar(NBI, smean[:, :], RST, -1.0 / D,
                                                                  op0=ALU.mult, op1=ALU.mult)),
           reads=["smean", f"rstd{i}"], writes=[f"nb{i}"])
        st("vector", (lambda v, RST=RST, RP1=RP1: v.tensor_scalar(RP1, RST, 1.0, None, op0=ALU.add)),
           reads=[f"rstd{i}"], writes=[f"rp1{i}"])
        st("scalar", (lambda s, XT=XT, RST=RST, NBI=NBI: s.activation(
            out=xnb[:, :], in_=XT[:, :], func=AF.Identity, bias=NBI, scale=RST)),
           reads=[xtn, f"nb{i}", f"rstd{i}"], writes=["xnb"])
    def a_stage1b(i):
        p = i % 2
        XN_T = xnT[p]
        xnTn = f"xnT{p}"
        st("tensor", (lambda t: [t.transpose(psT[:, k * TILE:(k + 1) * TILE],
                                             xnb[:, k * TILE:(k + 1) * TILE], idenb[:, :])
                                 for k in range(8)][-1]),
           reads=["xnb", "idenb"], writes=["psT"])
        if i == 0:
            st("vector", (lambda v, XN_T=XN_T: v.tensor_copy(
                xnT3(XN_T, 1, 2), xbound[:, :].rearrange("p (c k) -> p c k", k=1)[:, :, 0:1])),
               reads=["xbound"], writes=[xnTn])
        else:
            o = xnT[1 - p]
            st("vector", (lambda v, XN_T=XN_T, o=o: v.tensor_copy(
                xnT3(XN_T, 1, 2), xnT3(o, 129, 130))),
               reads=[f"xnT{1 - p}"], writes=[xnTn])
        st("scalar", (lambda s, XN_T=XN_T: s.copy(
            out=xnT3(XN_T, 2, 130),
            in_=psT[:, :].rearrange("p (c k) -> p c k", k=TILE))),
           reads=["psT"], writes=[xnTn])
        st("sync", (lambda s, i=i, XN_T=XN_T: s.dma_start(
            out=sp_xnT_d[i * TILE:(i + 1) * TILE, :], in_=XN_T[:, :])),
           reads=[xnTn], writes=[f"spx{i}"], dma=True)

    def a_stage2(i):
        p = i % 2
        XN_T = xnT[p]
        xnTn = f"xnT{p}"
        Pb = Pb_[p]
        Pbn = f"Pb{p}"
        st("tensor", (lambda t, XN_T=XN_T: em_proj(t, XN_T, 0, psA)),
           reads=[xnTn, "A_sb"], writes=["psA_h0", "psA_h1"])
        st("vector", (lambda v: v.tensor_copy(fqraw[:, :], psA[:, :])),
           reads=["psA_h0", "psA_h1"], writes=["fqraw"])
        st("tensor", (lambda t, XN_T=XN_T: em_proj(t, XN_T, D, psB)),
           reads=[xnTn, "A_sb"], writes=["psB_h0", "psB_h1"])
        st("vector", (lambda v: v.tensor_copy(fkraw[:, :], psB[:, :])),
           reads=["psB_h0", "psB_h1"], writes=["fkraw"])
        st("scalar", (lambda s: (
            s.activation(out=mg[:, 0:512], in_=fqraw[:, 0:512], func=AF.Square),
            s.activation(out=mg2[:, 0:511], in_=fqraw[:, 512:1023], func=AF.Square),
            s.activation(out=mg[:, 512:513], in_=fqraw[:, 1023:1024], func=AF.Square))[-1]),
           reads=["fqraw"], writes=["mg", "mg2"])
        st("vector", (lambda v: v.tensor_tensor(mg[:, 0:511], mg[:, 0:511], mg2[:, 0:511], op=ALU.add)),
           reads=["mg", "mg2"], writes=["mg"])
        st("scalar", (lambda s: s.activation(out=mg[:, 0:513], in_=mg[:, 0:513], func=AF.Ln)),
           reads=["mg"], writes=["mg"])
        st("scalar", (lambda s: s.activation(out=mg[:, 0:513], in_=mg[:, 0:513], func=AF.Exp, scale=0.5)),
           reads=["mg"], writes=["mg"])
        st("scalar", (lambda s: s.activation(out=mg[:, 0:513], in_=mg[:, 0:513], func=AF.Ln, bias=epsE[:, :])),
           reads=["mg", "epsE"], writes=["mg"])
        st("scalar", (lambda s: s.activation(out=inv_q[:, 0:513], in_=mg[:, 0:513], func=AF.Exp, scale=-1.0)),
           reads=["mg"], writes=["inv_q"])
        st("vector", (lambda v: (
            v.tensor_tensor(fqn[:, 0:512], fqraw[:, 0:512], inv_q[:, 0:512], op=ALU.mult),
            v.tensor_tensor(fqn[:, 512:1023], fqraw[:, 512:1023], inv_q[:, 0:511], op=ALU.mult),
            v.tensor_tensor(fqn[:, 1023:1024], fqraw[:, 1023:1024], inv_q[:, 512:513], op=ALU.mult))[-1]),
           reads=["fqraw", "inv_q"], writes=["fqn"])
        st("sync", (lambda s, i=i: s.dma_start(
            out=sp_fqn_d[i * TILE:(i + 1) * TILE, :], in_=fqn[:, :])),
           reads=["fqn"], writes=[f"spq{i}"], dma=True)
        st("scalar", (lambda s: (
            s.activation(out=mg[:, 0:512], in_=fkraw[:, 0:512], func=AF.Square),
            s.activation(out=mg2[:, 0:511], in_=fkraw[:, 512:1023], func=AF.Square),
            s.activation(out=mg[:, 512:513], in_=fkraw[:, 1023:1024], func=AF.Square))[-1]),
           reads=["fkraw", "mg", "mg2"], writes=["mg", "mg2"])
        st("vector", (lambda v: v.tensor_tensor(mg[:, 0:511], mg[:, 0:511], mg2[:, 0:511], op=ALU.add)),
           reads=["mg", "mg2"], writes=["mg"])
        st("scalar", (lambda s: s.activation(out=mg[:, 0:513], in_=mg[:, 0:513], func=AF.Ln)),
           reads=["mg"], writes=["mg"])
        st("scalar", (lambda s: s.activation(out=mg[:, 0:513], in_=mg[:, 0:513], func=AF.Exp, scale=0.5)),
           reads=["mg"], writes=["mg"])
        st("scalar", (lambda s: s.activation(out=mg[:, 0:513], in_=mg[:, 0:513], func=AF.Ln, bias=epsE[:, :])),
           reads=["mg", "epsE"], writes=["mg"])
        st("scalar", (lambda s: s.activation(out=inv_k[:, 0:513], in_=mg[:, 0:513], func=AF.Exp, scale=-1.0)),
           reads=["mg"], writes=["inv_k"])
        st("vector", (lambda v: (
            v.tensor_tensor(fkn[:, 0:512], fkraw[:, 0:512], inv_k[:, 0:512], op=ALU.mult),
            v.tensor_tensor(fkn[:, 512:1023], fkraw[:, 512:1023], inv_k[:, 0:511], op=ALU.mult),
            v.tensor_tensor(fkn[:, 1023:1024], fkraw[:, 1023:1024], inv_k[:, 512:513], op=ALU.mult))[-1]),
           reads=["fkraw", "inv_k"], writes=["fkn"])
        st("tensor", (lambda t, XN_T=XN_T: em_proj(t, XN_T, 2 * D, psA)),
           reads=[xnTn, "A_sb"], writes=["psA_h0", "psA_h1"])
        st("scalar", (lambda s: s.copy(out=fv[:, :], in_=psA[:, :])),
           reads=["psA_h0", "psA_h1"], writes=["fv"])
        st("vector", (lambda v: v.tensor_tensor(mg[:, 0:512], fkn[:, 0:512], fv[:, 0:512], op=ALU.mult)),
           reads=["fkn", "fv", "mg"], writes=["mg"])
        st("vector", (lambda v: v.tensor_tensor(mg2[:, 0:511], fkn[:, 512:1023], fv[:, 512:1023], op=ALU.mult)),
           reads=["fkn", "fv", "mg2"], writes=["mg2"])
        st("gpsimd", (lambda g: g.tensor_tensor(inv_q[:, 0:511], fkn[:, 0:511], fv[:, 512:1023], op=ALU.mult)),
           reads=["fkn", "fv", "inv_q"], writes=["inv_q"])
        st("gpsimd", (lambda g: (
            g.tensor_tensor(inv_k[:, 0:511], fkn[:, 512:1023], fv[:, 0:511], op=ALU.mult),
            g.memset(dumm[:, :], 0.0))[-1]),
           reads=["fkn", "fv", "inv_k"], writes=["inv_k"])
        st("vector", (lambda v: v.tensor_tensor(Pb[:, 0:511], mg[:, 0:511], mg2[:, 0:511], op=ALU.subtract)),
           reads=["mg", "mg2"], writes=[Pbn])
        st("vector", (lambda v: v.tensor_copy(Pb[:, 511:512], mg[:, 511:512])),
           reads=["mg"], writes=[Pbn])
        st("vector", (lambda v: v.tensor_tensor(Pb[:, 512:1023], inv_q[:, 0:511], inv_k[:, 0:511], op=ALU.add)),
           reads=["inv_q", "inv_k"], writes=[Pbn])
        st("vector", (lambda v: v.tensor_tensor(Pb[:, 1023:1024], fkn[:, 1023:1024], fv[:, 1023:1024], op=ALU.mult)),
           reads=["fkn", "fv"], writes=[Pbn])


    def a_tail(i):
        Pb = Pb_[i % 2]
        Pbn = f"Pb{i % 2}"
        def em_prefix(t):
            last = None
            for n in range(2):
                last = t.matmul(psC[:, n * 512:(n + 1) * 512], trilb[:, :],
                                Pb[:, n * 512:(n + 1) * 512], start=True, stop=True)
            return last

        st("tensor", em_prefix, reads=[Pbn, "trilb"], writes=["psC_h0", "psC_h1"])
        st("scalar", (lambda s: s.copy(out=fkvb[:, :], in_=psC[:, :])),
           reads=["psC_h0", "psC_h1"], writes=["fkvb"])
        st("gpsimd", (lambda g, i=i: g.dma_start(
            out=sp_fkv_d[i * TILE:(i + 1) * TILE, :], in_=fkvb[:, :])),
           reads=["fkvb"], writes=[f"spf{i}"], dma=True)
        if i == 0:
            st("vector", (lambda v: (
                v.tensor_copy(y1[:, 0:512], psC[:, 0:512]),
                v.tensor_copy(y1[:, 512:1024], psC[:, 512:1024]))[-1]),
               reads=["psC_h0", "psC_h1", "fkvb"], writes=["y1"])
        else:
            st("vector", (lambda v: (
                v.tensor_tensor(y1[:, 0:512], y1[:, 0:512], psC[:, 0:512], op=ALU.add),
                v.tensor_tensor(y1[:, 512:1024], y1[:, 512:1024], psC[:, 512:1024], op=ALU.add))[-1]),
               reads=["psC_h0", "psC_h1", "y1", "fkvb"], writes=["y1"])
        st("gpsimd", (lambda g, i=i: g.dma_start(out=sp_carry_d[i:i + 1, :], in_=y1[127:128, :])),
           reads=["y1"], writes=[f"spc{i}"], dma=True)

    st("scalar", (lambda s: _load_act_set(s, nc, "natural_log_exp_and_others")), writes=["acttab"])
    a_stage1(0)
    a_stage1b(0)
    a_stage1(1)
    a_stage1b(1)
    for i in range(NT):
        if i + 2 < NT:
            a_stage1(i + 2)
        a_stage2(i)
        if i > 0:
            a_tail(i - 1)
        if i + 2 < NT:
            a_stage1b(i + 2)
    a_tail(NT - 1)

    with nc.Block() as blkA:
        sch.replay(blkA, sems, final_wait=False)
    sch = Sched(sch.count, sch.bufs)
    st = sch.step

    _flush = {s: v for s, v in sch.count.items()
              if not s.endswith("_dmai") and v > 0}
    st("gpsimd", (lambda g: g.nop()), extra_waits=_flush)
    st("gpsimd", (lambda g: g.dma_start(out=cc_in.ap()[:, :], in_=y1[127:128, :])),
       reads=["y1"], writes=["cc_in"], dma=True)
    st("gpsimd", (lambda g: g.collective_compute(
        "AllGather", ALU.bypass, replica_groups=[list(range(8))],
        ins=[cc_in.ap().opt()], outs=[cc_out.ap().opt()])),
       reads=["cc_in"], writes=["cc_out"])
    st("gpsimd", (lambda g: g.dma_start(out=ag_sb[:, :], in_=cc_out.ap()[:, :])),
       reads=["cc_out"], writes=["ag_sb"], dma=True)

    def em_selmm(t):
        last = None
        for n in range(2):
            last = t.matmul(psC[0:1, n * 512:(n + 1) * 512], sel_sb[:, :],
                            ag_sb[:, n * 512:(n + 1) * 512], start=True, stop=True)
        return last

    st("tensor", em_selmm, reads=["sel_sb", "ag_sb"], writes=["psC_h0", "psC_h1"])
    st("vector", (lambda v: v.tensor_copy(carry_in_b[:, :], psC[0:1, :])),
       reads=["psC_h0", "psC_h1"], writes=["carry_in_b"])

    with nc.Block() as blkC:
        sch.replay(blkC, sems, final_wait=False)
    sch = Sched(sch.count, sch.bufs)
    st = sch.step

    st("scalar", (lambda s: _load_act_set(s, nc, "sigmoid_and_others")), writes=["acttab"])
    fkvB = [fkvb, fkvb2]
    fqnB = [fqn, fqn2]
    for i in range(NT):
        p = i % 2
        XT, XN_T = xt[p], xnT[p]
        xtn, xnTn = f"xt{p}", f"xnT{p}"
        FKV, FQN = fkvB[p], fqnB[p]
        fkvn, fqnn = f"fkvb{p}", f"fqn{p}"
        NBI = lnstats[:, 3 * i + 1:3 * i + 2]
        RP1 = lnstats[:, 3 * i + 2:3 * i + 3]

        st("sync", (lambda s, i=i, XT=XT: s.dma_start(
            out=XT[:, :], in_=x_d[i * TILE:(i + 1) * TILE, :])),
           writes=[xtn], dma=True)
        st("sync", (lambda s, i=i, XN_T=XN_T: s.dma_start(
            out=XN_T[:, :], in_=sp_xnT_d[i * TILE:(i + 1) * TILE, :])),
           reads=[f"spx{i}"], writes=[xnTn], dma=True)
        st("sync", (lambda s, i=i, FKV=FKV: s.dma_start(
            out=FKV[:, :], in_=sp_fkv_d[i * TILE:(i + 1) * TILE, :])),
           reads=[f"spf{i}"], writes=[fkvn], dma=True)
        st("sync", (lambda s, i=i, FQN=FQN: s.dma_start(
            out=FQN[:, :], in_=sp_fqn_d[i * TILE:(i + 1) * TILE, :])),
           reads=[f"spq{i}"], writes=[fqnn], dma=True)
        if i > 0:
            st("gpsimd", (lambda g, i=i, SNP=snap_b[i % 2]: g.dma_start(
                out=SNP[:, :], in_=sp_carry_d[i - 1:i, :])),
               reads=[f"spc{i - 1}"], writes=[f"snap_b{i % 2}"], dma=True)

        st("scalar", (lambda s, XT=XT, RP1=RP1, NBI=NBI: s.activation(
            out=y1[:, :], in_=XT[:, :], func=AF.Identity, bias=NBI, scale=RP1)),
           reads=[xtn, f"nb{i}", f"rp1{i}"], writes=["y1"])

        st("vector", (lambda v, XN_T=XN_T: v.tensor_tensor(
            dm[:, :].rearrange("p (c k) -> p c k", k=TILE),
            xnT3(XN_T, 2, 130), xnT3(XN_T, 1, 129), op=ALU.subtract)),
           reads=[xnTn], writes=["dm"])
        st("vector", (lambda v: v.tensor_tensor(b1[:, :], dm[:, :], MUr[:, :], op=ALU.mult)),
           reads=["dm", "MUr"], writes=["b1"])
        st("vector", (lambda v, XN_T=XN_T: v.tensor_tensor(
            xr[:, :].rearrange("p (c k) -> p c k", k=TILE),
            b1[:, :].rearrange("p (c k) -> p c k", k=TILE), xnT3(XN_T, 1, 129), op=ALU.add)),
           reads=["b1", xnTn], writes=["xr"])
        st("gpsimd", (lambda g: g.tensor_tensor(b2[:, :], dm[:, :], MUk[:, :], op=ALU.mult)),
           reads=["dm", "MUk"], writes=["b2"])
        st("gpsimd", (lambda g, XN_T=XN_T: (
            g.tensor_tensor(
                xk[:, :].rearrange("p (c k) -> p c k", k=TILE),
                b2[:, :].rearrange("p (c k) -> p c k", k=TILE), xnT3(XN_T, 1, 129), op=ALU.add),
            g.memset(dumm[:, :], 0.0))[-1]),
           reads=["b2", xnTn], writes=["xk"])

        def em_remat(t, i=i, FKV=FKV, SNP=snap_b[i % 2]):
            last = None
            for n in range(2):
                t.matmul(psC[:, n * 512:(n + 1) * 512], idenb[:, :],
                         FKV[:, n * 512:(n + 1) * 512], start=True, stop=False)
                if i > 0:
                    t.matmul(psC[:, n * 512:(n + 1) * 512], ones_row[:, :],
                             SNP[:, n * 512:(n + 1) * 512], start=False, stop=False)
                last = t.matmul(psC[:, n * 512:(n + 1) * 512], ones_row[:, :],
                                carry_in_b[:, n * 512:(n + 1) * 512], start=False, stop=True)
            return last

        st("tensor", em_remat,
           reads=[fkvn, f"snap_b{i % 2}", "carry_in_b", "idenb", "ones_row"],
           writes=["psC_h0", "psC_h1"])
        st("scalar", (lambda s, FKV=FKV: s.copy(out=FKV[:, :], in_=psC[:, :])),
           reads=["psC_h0", "psC_h1"], writes=[fkvn])

        st("tensor", (lambda t: em_ffn_mm(t, xr, WrT, psB)),
           reads=["xr", "WrT"], writes=["psB_h0", "psB_h1"])
        st("scalar", (lambda s: s.activation(out=r_sb[:, :], in_=psB[:, :], func=AF.Sigmoid)),
           reads=["psB_h0", "psB_h1"], writes=["r_sb"])
        st("tensor", (lambda t: em_ffn_mm(t, xk, Wk2T, psA)),
           reads=["xk", "Wk2T", "fkvb"], writes=["psA_h0", "psA_h1"])
        st("scalar", (lambda s: s.activation(out=kk[:, :], in_=psA[:, :], func=AF.Relu)),
           reads=["psA_h0", "psA_h1"], writes=["kk"])

        st("vector", (lambda v, FKV=FKV, FQN=FQN: v.tensor_tensor(mg[:, 0:512], FKV[:, 0:512], FQN[:, 0:512], op=ALU.mult)),
           reads=[fkvn, fqnn, "mg"], writes=["mg"])
        st("vector", (lambda v, FKV=FKV, FQN=FQN: v.tensor_tensor(mg2[:, 0:511], FKV[:, 512:1023], FQN[:, 512:1023], op=ALU.mult)),
           reads=[fkvn, fqnn, "mg2"], writes=["mg2"])
        st("vector", (lambda v, FKV=FKV, FQN=FQN: v.tensor_tensor(inv_q[:, 0:511], FKV[:, 512:1023], FQN[:, 0:511], op=ALU.mult)),
           reads=[fkvn, fqnn, "inv_q"], writes=["inv_q"])
        st("vector", (lambda v, FKV=FKV, FQN=FQN: v.tensor_tensor(inv_k[:, 0:511], FKV[:, 0:511], FQN[:, 512:1023], op=ALU.mult)),
           reads=[fkvn, fqnn, "inv_k"], writes=["inv_k"])
        st("vector", (lambda v: v.tensor_tensor(Zb[:, 0:511], mg[:, 0:511], mg2[:, 0:511], op=ALU.add)),
           reads=["mg", "mg2"], writes=["Zb"])
        st("vector", (lambda v: v.tensor_copy(Zb[:, 511:512], mg[:, 511:512])),
           reads=["mg"], writes=["Zb"])
        st("vector", (lambda v: v.tensor_tensor(Zb[:, 512:1023], inv_q[:, 0:511], inv_k[:, 0:511], op=ALU.subtract)),
           reads=["inv_q", "inv_k"], writes=["Zb"])
        st("vector", (lambda v, FKV=FKV, FQN=FQN: v.tensor_tensor(Zb[:, 1023:1024], FKV[:, 1023:1024], FQN[:, 1023:1024], op=ALU.mult)),
           reads=[fkvn, fqnn], writes=["Zb"])

        st("tensor", (lambda t: [t.transpose(psT[:, k * TILE:(k + 1) * TILE],
                                             Zb[:, k * TILE:(k + 1) * TILE], idenb[:, :])
                                 for k in range(8)][-1]),
           reads=["Zb", "idenb"], writes=["psT"])
        st("scalar", (lambda s: s.copy(out=ZT[:, :], in_=psT[:, :])), reads=["psT"], writes=["ZT"])

        st("vector", (lambda v: v.tensor_tensor(Zb[:, :], kk[:, :], kk[:, :], op=ALU.mult)),
           reads=["kk", "Zb"], writes=["Zb"])
        st("tensor", (lambda t: [t.transpose(psT[:, k * TILE:(k + 1) * TILE],
                                             Zb[:, k * TILE:(k + 1) * TILE], idenb[:, :])
                                 for k in range(8)][-1]),
           reads=["Zb", "idenb", "ZT"], writes=["psT"])
        st("scalar", (lambda s: s.copy(out=kk[:, :], in_=psT[:, :])), reads=["psT"], writes=["kk"])

        def em_irfft(t):
            last = None
            for n in range(2):
                for kc in range(8):
                    last = t.matmul(
                        psB[:, n * 512:(n + 1) * 512],
                        ZT[:, kc * TILE:(kc + 1) * TILE],
                        C_sb[:, kc * D + n * 512:kc * D + (n + 1) * 512],
                        start=(kc == 0), stop=(kc == 7))
            return last

        st("tensor", em_irfft, reads=["ZT", "C_sb", "r_sb"], writes=["psB_h0", "psB_h1"])
        st("vector", (lambda v: v.tensor_tensor(y_sb[:, :], y1[:, :], psB[:, :], op=ALU.add)),
           reads=["y1", "psB_h0", "psB_h1", "y_stored"], writes=["y_sb"])

        st("tensor", (lambda t: em_ffn_mm(t, kk, Wv2T, psA)),
           reads=["kk", "Wv2T", "kk"], writes=["psA_h0", "psA_h1"])
        st("vector", (lambda v: v.tensor_tensor(g_s[:, :], r_sb[:, :], psA[:, :], op=ALU.mult)),
           reads=["r_sb", "psA_h0", "psA_h1", "g_s"], writes=["g_s"])
        st("vector", (lambda v: v.tensor_tensor(y_sb[:, :], y_sb[:, :], g_s[:, :], op=ALU.add)),
           reads=["y_sb", "g_s"], writes=["y_sb"])

        st("scalar", (lambda s, i=i: s.dma_start(out=y_d[i * TILE:(i + 1) * TILE, :], in_=y_sb[:, :])),
           reads=["y_sb"], writes=["y_stored"], dma=True)

    with nc.Block() as block:
        sch.replay(block, sems, final_wait=True)

    es.close()
    return nc


def kernel(L0_attn_x, L0_attn_Wq, L0_attn_Wk, L0_attn_Wv,
           L0_ffn_W_r, L0_ffn_W_k, L0_ffn_W_v, L0_ffn_mu_r, L0_ffn_mu_k):
    x = np.ascontiguousarray(np.asarray(L0_attn_x, dtype=np.float32))
    mpack, cpack, tri, iden = _host_constants()
    common = {
        "wq": np.ascontiguousarray(L0_attn_Wq, dtype=np.float32),
        "wk": np.ascontiguousarray(L0_attn_Wk, dtype=np.float32),
        "wv": np.ascontiguousarray(L0_attn_Wv, dtype=np.float32),
        "wrt": np.ascontiguousarray(np.asarray(L0_ffn_W_r, dtype=np.float32).T),
        "wk2t": np.ascontiguousarray(np.asarray(L0_ffn_W_k, dtype=np.float32).T),
        "wv2t": np.ascontiguousarray(np.asarray(L0_ffn_W_v, dtype=np.float32).T),
        "mur": np.ascontiguousarray(np.asarray(L0_ffn_mu_r, dtype=np.float32).reshape(8, TILE).T),
        "muk": np.ascontiguousarray(np.asarray(L0_ffn_mu_k, dtype=np.float32).reshape(8, TILE).T),
        "mpack": mpack,
        "cpack": cpack,
        "tri": tri,
        "iden": iden,
    }
    in_maps = []
    for c in range(NCORES):
        b, h = c // 2, c % 2
        im = dict(common)
        im["x"] = np.ascontiguousarray(x[b, h * TCORE:(h + 1) * TCORE])
        im["xprev"] = (np.zeros((1, D), dtype=np.float32) if h == 0
                       else np.ascontiguousarray(x[b, TCORE - 1:TCORE]))
        sel = np.zeros((8, 1), dtype=np.float32)
        if h == 1:
            sel[c - 1, 0] = 1.0
        im["sel"] = sel
        in_maps.append(im)
    nc = build_program()
    res = run_bass_kernel_spmd(nc, in_maps, core_ids=list(range(NCORES)))
    out = np.empty((BATCH, SEQ, D), dtype=np.float32)
    for c in range(NCORES):
        b, h = c // 2, c % 2
        out[b, h * TCORE:(h + 1) * TCORE] = res.results[c]["y"]
    return out


# version 32 (final)
# speedup vs baseline: 1.0343x; 1.0343x over previous
import numpy as np
import concourse.bass as bass
import concourse.mybir as mybir
from concourse.bass_utils import run_bass_kernel_spmd
from contextlib import ExitStack

F32 = mybir.dt.float32
BF16 = mybir.dt.bfloat16
AF = mybir.ActivationFunctionType
ALU = mybir.AluOpType

D = 1024
SEQ = 8192
BATCH = 4
NCORES = 8
TILE = 128
TCORE = SEQ // 2
NT = TCORE // TILE
EPS_LN = 1e-5
EPS_FFT = 1e-8

ENGINES = ("sync", "gpsimd", "scalar", "vector", "tensor")


def _host_constants():
    NF = 513
    f = np.arange(NF)
    e = np.arange(D)
    ang = 2.0 * np.pi * np.outer(e, f) / D
    Mre = np.cos(ang)
    Mim = -np.sin(ang)
    mpack = np.concatenate([Mre[:, 1:513], Mim[:, 1:512], Mre[:, 0:1]], axis=1)
    d_ = np.arange(D)
    ang2 = 2.0 * np.pi * np.outer(f, d_) / D
    cf = np.ones(NF) * 2.0
    cf[0] = 1.0
    cf[512] = 1.0
    Cre = (cf[:, None] * np.cos(ang2)) / D
    Cim = (-2.0 * np.sin(ang2[1:512])) / D
    cpack = np.concatenate([Cre[1:513], Cim, Cre[0:1]], axis=0)
    tri = np.triu(np.ones((TILE, TILE)))
    iden = np.eye(TILE)
    return (
        np.ascontiguousarray(mpack, dtype=np.float32),
        np.ascontiguousarray(cpack, dtype=np.float32),
        np.ascontiguousarray(tri, dtype=np.float32),
        np.ascontiguousarray(iden, dtype=np.float32),
    )


class Sched:

    DMA_POOL = 6

    def __init__(self, start=None, bufs=None):
        self.q = {e: [] for e in ENGINES}
        self.count = dict(start) if start else {}
        self.bufs = dict(bufs) if bufs else {}

    def _bump(self, sem, inc):
        v = self.count.get(sem, 0) + inc
        self.count[sem] = v
        return v

    def step(self, eng, emit, reads=(), writes=(), dma=False, extra_waits=None):
        waits = {}
        if extra_waits:
            waits.update(extra_waits)

        def dep(sem, val):
            waits[sem] = max(waits.get(sem, 0), val)

        for b in reads:
            rec = self.bufs.get(b)
            if rec and rec[0]:
                dep(*rec[0])
        for b in writes:
            rec = self.bufs.get(b)
            if rec:
                if rec[0]:
                    dep(*rec[0])
                for r in rec[1]:
                    dep(*r)
        if dma:
            idx = self.count.get(f"{eng}_dmai", 0)
            self.count[f"{eng}_dmai"] = idx + 1
            sem = f"{eng}_d{idx % self.DMA_POOL}"
            cur = self.count.get(sem, 0)
            if cur > 0:
                dep(sem, cur)
            val = self._bump(sem, 16)
        else:
            sem = eng
            val = self._bump(sem, 1)
        self.q[eng].append((emit, dict(waits), sem, 16 if dma else 1))
        for b in reads:
            self.bufs.setdefault(b, [None, []])[1].append((sem, val))
        for b in writes:
            self.bufs[b] = [(sem, val), []]
        return (sem, val)

    def replay(self, block, sems, final_wait=True):
        waited = {e: {} for e in ENGINES}

        def body_for(name):
            def body(engobj):
                w = waited[name]
                for emit, waits, sem, inc in self.q[name]:
                    for f, v in waits.items():
                        if w.get(f, 0) < v:
                            engobj.wait_ge(sems[f], v)
                            w[f] = v
                    last = emit(engobj)
                    last.then_inc(sems[sem], inc)
                if final_wait and name in ("sync", "gpsimd"):
                    for j in range(Sched.DMA_POOL):
                        s = f"{name}_d{j}"
                        v = self.count.get(s, 0)
                        if v > 0 and waited[name].get(s, 0) < v:
                            engobj.wait_ge(sems[s], v)

            return body

        for name in ENGINES:
            if self.q[name]:
                getattr(block, name)(body_for(name))


def _load_act_set(s, nc, set_name):
    from concourse.hw_specs import get_activation_tables
    idx = list(get_activation_tables(nc.m.arch)).index(set_name)
    return s.add_instruction(mybir.InstLoadActFuncSet(
        name=nc.get_next_instruction_name(), act_func_set_id=idx, ins=[], outs=[]))


def build_program():
    nc = bass.Bass("TRN2", target_bir_lowering=False)

    x_d = nc.dram_tensor("x", [TCORE, D], F32, kind="ExternalInput").ap()
    xprev_d = nc.dram_tensor("xprev", [1, D], F32, kind="ExternalInput").ap()
    sel_d = nc.dram_tensor("sel", [8, 1], F32, kind="ExternalInput").ap()
    wq_d = nc.dram_tensor("wq", [D, D], F32, kind="ExternalInput").ap()
    wk_d = nc.dram_tensor("wk", [D, D], F32, kind="ExternalInput").ap()
    wv_d = nc.dram_tensor("wv", [D, D], F32, kind="ExternalInput").ap()
    wrt_d = nc.dram_tensor("wrt", [D, D], F32, kind="ExternalInput").ap()
    wk2t_d = nc.dram_tensor("wk2t", [D, D], F32, kind="ExternalInput").ap()
    wv2t_d = nc.dram_tensor("wv2t", [D, D], F32, kind="ExternalInput").ap()
    mur_d = nc.dram_tensor("mur", [TILE, 8], F32, kind="ExternalInput").ap()
    muk_d = nc.dram_tensor("muk", [TILE, 8], F32, kind="ExternalInput").ap()
    mpack_d = nc.dram_tensor("mpack", [D, D], F32, kind="ExternalInput").ap()
    cpack_d = nc.dram_tensor("cpack", [D, D], F32, kind="ExternalInput").ap()
    tri_d = nc.dram_tensor("tri", [TILE, TILE], F32, kind="ExternalInput").ap()
    iden_d = nc.dram_tensor("iden", [TILE, TILE], F32, kind="ExternalInput").ap()
    y_d = nc.dram_tensor("y", [TCORE, D], F32, kind="ExternalOutput").ap()
    sp_xnT_d = nc.dram_tensor("sp_xnT", [NT * TILE, 8 * 130], BF16).ap()
    sp_fkv_d = nc.dram_tensor("sp_fkv", [NT * TILE, D], BF16).ap()
    sp_fqn_d = nc.dram_tensor("sp_fqn", [NT * TILE, D], BF16).ap()
    sp_carry_d = nc.dram_tensor("sp_carry", [NT, D], F32).ap()
    cc_in = nc.dram_tensor("cc_in", [1, D], F32)
    cc_out = nc.dram_tensor("cc_out", [8, D], F32)

    es = ExitStack()

    def S(name, shape, dt):
        return es.enter_context(nc.sbuf_tensor(name, shape, dt))

    def Ps(name, shape, dt):
        return es.enter_context(nc.psum_tensor(name, shape, dt))

    A_sb = S("A_sb", [TILE, 8 * 3 * D], BF16)
    C_sb = S("C_sb", [TILE, 8 * D], BF16)
    WrT = S("WrT", [TILE, 8 * D], BF16)
    Wk2T = S("Wk2T", [TILE, 8 * D], BF16)
    Wv2T = S("Wv2T", [TILE, 8 * D], BF16)
    trilb = S("trilb", [TILE, TILE], BF16)
    iden = S("iden_sb", [TILE, TILE], F32)
    idenb = S("idenb", [TILE, TILE], BF16)
    mur = S("mur_sb", [TILE, 8], F32)
    muk = S("muk_sb", [TILE, 8], F32)
    MUr = S("MUr", [TILE, 8 * TILE], BF16)
    MUk = S("MUk", [TILE, 8 * TILE], BF16)
    onesb = S("onesb", [TILE, TILE], BF16)
    ones_col = S("ones_col", [TILE, 1], BF16)
    ones_row = S("ones_row", [1, TILE], BF16)
    epsLN = S("epsLN", [TILE, 1], F32)
    epsE = S("epsE", [TILE, 1], F32)
    zerof = S("zerof", [TILE, 1], F32)
    dumm = S("dumm", [1, 1], F32)
    carry_in_b = S("carry_in_b", [1, D], BF16)
    xbound = S("xbound", [TILE, 8], BF16)
    lnstats = S("lnstats", [TILE, 3 * NT], F32)
    sel_sb = S("sel_sb", [8, 1], F32)
    ag_sb = S("ag_sb", [8, D], F32)

    psA = Ps("psA", [TILE, D], F32)
    psB = Ps("psB", [TILE, D], F32)
    psT = Ps("psT", [TILE, D], BF16)
    psC = Ps("psC", [TILE, D], F32)

    sems = {e: es.enter_context(nc.semaphore(f"sem_{e}")) for e in ENGINES}
    for qe in ("sync", "gpsimd", "scalar"):
        for j in range(Sched.DMA_POOL):
            sems[f"{qe}_d{j}"] = es.enter_context(nc.semaphore(f"sem_{qe}_d{j}"))

    es_pro = ExitStack()
    Mb = es_pro.enter_context(nc.sbuf_tensor("Mb", [TILE, 8 * D], BF16))
    Wb = es_pro.enter_context(nc.sbuf_tensor("Wb", [TILE, 8 * D], BF16))
    stg = [es_pro.enter_context(nc.sbuf_tensor(f"stg{j}", [TILE, D], F32)) for j in range(6)]
    pvrow = es_pro.enter_context(nc.sbuf_tensor("pvrow", [1, D], F32))
    pvn = es_pro.enter_context(nc.sbuf_tensor("pvn", [1, D], F32))
    pv_ss = es_pro.enter_context(nc.sbuf_tensor("pv_ss", [1, 2], F32))
    pv_st = es_pro.enter_context(nc.sbuf_tensor("pv_st", [1, 6], F32))

    sch = Sched()
    st = sch.step

    def em_memsets(g):
        g.memset(onesb[:, :], 1.0)
        g.memset(ones_col[:, :], 1.0)
        g.memset(ones_row[:, :], 1.0)
        g.memset(epsLN[:, :], EPS_LN)
        g.memset(epsE[:, :], EPS_FFT)
        return g.memset(zerof[:, :], 0.0)

    st("gpsimd", em_memsets,
       writes=["onesb", "ones_col", "ones_row", "epsLN", "epsE", "zerof"])

    st("sync", (lambda s: s.dma_start(out=iden[:, :], in_=iden_d[:, :])), writes=["iden"], dma=True)
    st("sync", (lambda s: s.dma_start(out=mur[:, :], in_=mur_d[:, :])), writes=["mur"], dma=True)
    st("sync", (lambda s: s.dma_start(out=muk[:, :], in_=muk_d[:, :])), writes=["muk"], dma=True)
    st("sync", (lambda s: s.dma_start(out=pvrow[:, :], in_=xprev_d[:, :])), writes=["pvrow"], dma=True)
    st("sync", (lambda s: s.dma_start(out=sel_sb[:, :], in_=sel_d[:, :])), writes=["sel_sb"], dma=True)

    _lc = [0]

    def load_cast(dst, dst_off, srcap, rows, dsttok):
        j = _lc[0] % 6
        _lc[0] += 1
        st("sync", (lambda s, j=j: s.dma_start(out=stg[j][:, :],
                                               in_=srcap[rows:rows + TILE, :])),
           writes=[f"stg{j}"], dma=True)
        eng = "vector" if (_lc[0] % 2 == 0) else "scalar"
        if eng == "vector":
            st("vector", (lambda v, j=j: v.tensor_copy(dst[:, dst_off:dst_off + D], stg[j][:, :])),
               reads=[f"stg{j}"], writes=[dsttok])
        else:
            st("scalar", (lambda s, j=j: s.copy(out=dst[:, dst_off:dst_off + D], in_=stg[j][:, :])),
               reads=[f"stg{j}"], writes=[dsttok])

    st("sync", (lambda s: s.dma_start(out=stg[0][:, 0:TILE], in_=tri_d[:, :])),
       writes=["stg0"], dma=True)
    st("vector", (lambda v: v.tensor_copy(trilb[:, :], stg[0][:, 0:TILE])),
       reads=["stg0"], writes=["trilb"])
    for k in range(8):
        load_cast(Mb, k * D, mpack_d, k * TILE, f"Mb{k}")
    st("scalar", (lambda s: s.copy(out=idenb[:, :], in_=iden[:, :])),
       reads=["iden"], writes=["idenb"])

    for nm, dst, src, srctok in (("MUr", MUr, mur, "mur"), ("MUk", MUk, muk, "muk")):
        for k in range(8):
            st("scalar",
               (lambda s, dst=dst, src=src, k=k: s.activation(
                   out=dst[:, k * TILE:(k + 1) * TILE], in_=onesb[:, :],
                   func=AF.Identity, bias=zerof[:, :], scale=src[:, k:k + 1])),
               reads=["onesb", srctok, "zerof"], writes=[nm])

    for wi, (wsrc, acol) in enumerate(((wq_d, 0), (wk_d, D), (wv_d, 2 * D))):
        for k in range(8):
            load_cast(Wb, k * D, wsrc, k * TILE, f"Wb{k}")
        for m in range(8):
            ps = psA if (m % 2 == 0) else psB
            pstok = "psA" if (m % 2 == 0) else "psB"
            for n in range(2):
                def em_prep(t, m=m, n=n, ps=ps):
                    last = None
                    for kc in range(8):
                        last = t.matmul(
                            ps[:, n * 512:(n + 1) * 512],
                            Wb[:, kc * D + m * TILE:kc * D + (m + 1) * TILE],
                            Mb[:, kc * D + n * 512:kc * D + (n + 1) * 512],
                            start=(kc == 0), stop=(kc == 7))
                    return last

                st("tensor", em_prep,
                   reads=[f"Wb{k}" for k in range(8)] + [f"Mb{k}" for k in range(8)],
                   writes=[f"{pstok}_h{n}"])
                col = m * 3 * D + acol + n * 512
                if n == 0:
                    st("vector",
                       (lambda v, ps=ps, col=col: v.tensor_copy(
                           A_sb[:, col:col + 512], ps[:, 0:512])),
                       reads=[f"{pstok}_h0"], writes=["A_sb"])
                else:
                    st("scalar",
                       (lambda s, ps=ps, col=col: s.copy(
                           out=A_sb[:, col:col + 512], in_=ps[:, 512:1024])),
                       reads=[f"{pstok}_h1"], writes=["A_sb"])

    for k in range(8):
        load_cast(C_sb, k * D, cpack_d, k * TILE, "C_sb")
    for name, dst, wsrc in (("WrT", WrT, wrt_d), ("Wk2T", Wk2T, wk2t_d), ("Wv2T", Wv2T, wv2t_d)):
        for k in range(8):
            load_cast(dst, k * D, wsrc, k * TILE, name)
    st("scalar", (lambda s: (
        s.activation(out=pvn[:, :], in_=pvrow[:, :], func=AF.Square, accum_out=pv_ss[:, 0:1]),
        s.activation(out=pvn[:, :], in_=pvrow[:, :], func=AF.Copy, accum_out=pv_ss[:, 1:2]),
        s.copy(out=dumm[:, :], in_=zerof[0:1, :]))[-1]),
       reads=["pvrow"], writes=["pv_ss", "pvn"])
    st("vector", (lambda v: v.tensor_scalar(pv_st[:, 0:1], pv_ss[:, 1:2], 1.0 / D, None, op0=ALU.mult)),
       reads=["pv_ss"], writes=["pv_m"])
    st("vector", (lambda v: v.tensor_scalar(pv_st[:, 1:2], pv_ss[:, 0:1], 1.0 / D, None, op0=ALU.mult)),
       reads=["pv_ss"], writes=["pv_x2"])
    st("vector", (lambda v: v.tensor_tensor(pv_st[:, 2:3], pv_st[:, 0:1], pv_st[:, 0:1], op=ALU.mult)),
       reads=["pv_m"], writes=["pv_m2"])
    st("vector", (lambda v: v.tensor_tensor(pv_st[:, 1:2], pv_st[:, 1:2], pv_st[:, 2:3], op=ALU.subtract)),
       reads=["pv_x2", "pv_m2"], writes=["pv_var"])
    st("scalar", (lambda s: s.activation(out=pv_st[:, 3:4], in_=pv_st[:, 1:2], func=AF.Sqrt,
                                         bias=epsLN[0:1, :])),
       reads=["pv_var", "epsLN"], writes=["pv_sd"])
    st("vector", (lambda v: v.reciprocal(pv_st[:, 3:4], pv_st[:, 3:4])),
       reads=["pv_sd"], writes=["pv_rs"])
    st("vector", (lambda v: v.tensor_scalar(pv_st[:, 4:5], pv_st[:, 0:1], pv_st[:, 3:4], -1.0,
                                            op0=ALU.mult, op1=ALU.mult)),
       reads=["pv_m", "pv_rs"], writes=["pv_nb"])
    st("scalar", (lambda s: s.activation(out=pvn[:, :], in_=pvrow[:, :], func=AF.Identity,
                                         bias=pv_st[0:1, 4:5], scale=pv_st[0:1, 3:4])),
       reads=["pvrow", "pv_nb", "pv_rs", "pvn"], writes=["pvn"])
    st("tensor", (lambda t: [t.transpose(psA[:, k:k + 1], pvn[0:1, k * TILE:(k + 1) * TILE],
                                         iden[0:1, 0:1]) for k in range(8)][-1]),
       reads=["pvn", "iden", "psA_h0", "psA_h1"], writes=["psA_h0"])
    st("vector", (lambda v: v.tensor_copy(xbound[:, :], psA[:, 0:8])),
       reads=["psA_h0"], writes=["xbound"])

    with nc.Block() as blk1:
        sch.replay(blk1, sems, final_wait=False)
    es_pro.close()
    _carry_bufs = sch.bufs

    xt = [S(f"xt{p}", [TILE, D], F32) for p in range(2)]
    ssum = S("ssum", [TILE, 1], F32)
    smean = S("smean", [TILE, 1], F32)
    mean = S("mean", [TILE, 1], F32)
    msq = S("msq", [TILE, 1], F32)
    var = S("var", [TILE, 1], F32)
    sd = S("sd", [TILE, 1], F32)
    xnb = S("xnb", [TILE, D], BF16)
    sq_scr = S("sq_scr", [TILE, D], BF16)
    g_s = S("g_s", [TILE, D], F32)
    xnT = [S(f"xnT{p}", [TILE, 8 * 130], BF16) for p in range(2)]
    mg = S("mg", [TILE, 516], F32)
    mg2 = S("mg2", [TILE, 512], F32)
    inv_q = S("inv_q", [TILE, 516], F32)
    inv_k = S("inv_k", [TILE, 516], F32)
    fqn = S("fqn", [TILE, D], BF16)
    fkn = S("fkn", [TILE, D], BF16)
    fv = S("fv", [TILE, D], BF16)
    Pb_ = [S(f"Pb{p}", [TILE, D], BF16) for p in range(2)]
    fkvb = S("fkvb", [TILE, D], BF16)
    fkvb2 = S("fkvb2", [TILE, D], BF16)
    fqn2 = S("fqn2", [TILE, D], BF16)
    fqraw = S("fqraw", [TILE, D], BF16)
    fkraw = S("fkraw", [TILE, D], BF16)
    Zb = S("Zb", [TILE, D], BF16)
    ZT = S("ZT", [TILE, D], BF16)
    dm = S("dm", [TILE, 8 * TILE], BF16)
    b1 = S("b1", [TILE, 8 * TILE], BF16)
    b2 = S("b2", [TILE, 8 * TILE], BF16)
    xr = S("xr", [TILE, 8 * TILE], BF16)
    xk = S("xk", [TILE, 8 * TILE], BF16)
    r_sb = S("r_sb", [TILE, D], F32)
    kk = S("kk", [TILE, D], BF16)
    y1 = S("y1", [TILE, D], F32)
    y_sb = S("y_sb", [TILE, D], F32)
    snap_b = [S(f"snap_b{p}", [1, D], BF16) for p in range(2)]

    sch = Sched(sch.count, _carry_bufs)
    st = sch.step

    def xnT3(buf, lo, hi):
        return buf[:, :].rearrange("p (c k) -> p c k", k=130)[:, :, lo:hi]

    def em_proj(t, XN_T, acol, ps):
        last = None
        for n in range(2):
            for kc in range(8):
                last = t.matmul(
                    ps[:, n * 512:(n + 1) * 512],
                    XN_T[:, kc * 130 + 2:kc * 130 + 130],
                    A_sb[:, kc * 3 * D + acol + n * 512:kc * 3 * D + acol + (n + 1) * 512],
                    start=(kc == 0), stop=(kc == 7))
        return last

    def em_ffn_mm(t, inb, wsb, ps):
        last = None
        for n in range(2):
            for kc in range(8):
                last = t.matmul(
                    ps[:, n * 512:(n + 1) * 512],
                    inb[:, kc * TILE:(kc + 1) * TILE],
                    wsb[:, kc * D + n * 512:kc * D + (n + 1) * 512],
                    start=(kc == 0), stop=(kc == 7))
        return last

    def a_stage1(i):
        p = i % 2
        XT, XN_T = xt[p], xnT[p]
        xtn, xnTn = f"xt{p}", f"xnT{p}"
        st("sync", (lambda s, i=i, XT=XT: s.dma_start(
            out=XT[:, :], in_=x_d[i * TILE:(i + 1) * TILE, :])),
           writes=[xtn], dma=True)
        RST = lnstats[:, 3 * i:3 * i + 1]
        NBI = lnstats[:, 3 * i + 1:3 * i + 2]
        RP1 = lnstats[:, 3 * i + 2:3 * i + 3]
        st("scalar", (lambda s, XT=XT: (
            s.activation(out=sq_scr[:, :], in_=XT[:, :], func=AF.Square, accum_out=ssum[:, :]),
            s.activation(out=sq_scr[:, :], in_=XT[:, :], func=AF.Copy, accum_out=smean[:, :]),
            s.copy(out=dumm[:, :], in_=zerof[0:1, :]))[-1]),
           reads=[xtn], writes=["ssum", "smean", "sq_scr"])
        st("vector", (lambda v: v.tensor_scalar(msq[:, :], smean[:, :], smean[:, :], 1.0 / (D * D),
                                                op0=ALU.mult, op1=ALU.mult)),
           reads=["smean"], writes=["msq"])
        st("vector", (lambda v: v.scalar_tensor_tensor(var[:, :], ssum[:, :], 1.0 / D, msq[:, :],
                                                       op0=ALU.mult, op1=ALU.subtract)),
           reads=["ssum", "msq"], writes=["var"])
        st("scalar", (lambda s: s.activation(out=sd[:, :], in_=var[:, :], func=AF.Ln, bias=epsLN[:, :])),
           reads=["var", "epsLN"], writes=["sd"])
        st("scalar", (lambda s: s.activation(out=sd[:, :], in_=sd[:, :], func=AF.Exp, scale=0.5)),
           reads=["sd"], writes=["sd"])
        st("vector", (lambda v, RST=RST: v.reciprocal(RST, sd[:, :])),
           reads=["sd"], writes=[f"rstd{i}"])
        st("vector", (lambda v, RST=RST, NBI=NBI: v.tensor_scalar(NBI, smean[:, :], RST, -1.0 / D,
                                                                  op0=ALU.mult, op1=ALU.mult)),
           reads=["smean", f"rstd{i}"], writes=[f"nb{i}"])
        st("vector", (lambda v, RST=RST, RP1=RP1: v.tensor_scalar(RP1, RST, 1.0, None, op0=ALU.add)),
           reads=[f"rstd{i}"], writes=[f"rp1{i}"])
        st("scalar", (lambda s, XT=XT, RST=RST, NBI=NBI: s.activation(
            out=xnb[:, :], in_=XT[:, :], func=AF.Identity, bias=NBI, scale=RST)),
           reads=[xtn, f"nb{i}", f"rstd{i}"], writes=["xnb"])
    def a_stage1b(i):
        p = i % 2
        XN_T = xnT[p]
        xnTn = f"xnT{p}"
        st("tensor", (lambda t: [t.transpose(psT[:, k * TILE:(k + 1) * TILE],
                                             xnb[:, k * TILE:(k + 1) * TILE], idenb[:, :])
                                 for k in range(8)][-1]),
           reads=["xnb", "idenb"], writes=["psT"])
        if i == 0:
            st("vector", (lambda v, XN_T=XN_T: v.tensor_copy(
                xnT3(XN_T, 1, 2), xbound[:, :].rearrange("p (c k) -> p c k", k=1)[:, :, 0:1])),
               reads=["xbound"], writes=[xnTn])
        else:
            o = xnT[1 - p]
            st("vector", (lambda v, XN_T=XN_T, o=o: v.tensor_copy(
                xnT3(XN_T, 1, 2), xnT3(o, 129, 130))),
               reads=[f"xnT{1 - p}"], writes=[xnTn])
        st("scalar", (lambda s, XN_T=XN_T: s.copy(
            out=xnT3(XN_T, 2, 130),
            in_=psT[:, :].rearrange("p (c k) -> p c k", k=TILE))),
           reads=["psT"], writes=[xnTn])
        st("sync", (lambda s, i=i, XN_T=XN_T: s.dma_start(
            out=sp_xnT_d[i * TILE:(i + 1) * TILE, :], in_=XN_T[:, :])),
           reads=[xnTn], writes=[f"spx{i}"], dma=True)

    def a_stage2(i):
        p = i % 2
        XN_T = xnT[p]
        xnTn = f"xnT{p}"
        Pb = Pb_[p]
        Pbn = f"Pb{p}"
        st("tensor", (lambda t, XN_T=XN_T: em_proj(t, XN_T, 0, psA)),
           reads=[xnTn, "A_sb"], writes=["psA_h0", "psA_h1"])
        st("vector", (lambda v: v.tensor_copy(fqraw[:, :], psA[:, :])),
           reads=["psA_h0", "psA_h1"], writes=["fqraw"])
        st("tensor", (lambda t, XN_T=XN_T: em_proj(t, XN_T, D, psB)),
           reads=[xnTn, "A_sb"], writes=["psB_h0", "psB_h1"])
        st("vector", (lambda v: v.tensor_copy(fkraw[:, :], psB[:, :])),
           reads=["psB_h0", "psB_h1"], writes=["fkraw"])
        st("scalar", (lambda s: (
            s.activation(out=mg[:, 0:512], in_=fqraw[:, 0:512], func=AF.Square),
            s.activation(out=mg2[:, 0:511], in_=fqraw[:, 512:1023], func=AF.Square),
            s.activation(out=mg[:, 512:513], in_=fqraw[:, 1023:1024], func=AF.Square))[-1]),
           reads=["fqraw"], writes=["mg", "mg2"])
        st("vector", (lambda v: v.tensor_tensor(mg[:, 0:511], mg[:, 0:511], mg2[:, 0:511], op=ALU.add)),
           reads=["mg", "mg2"], writes=["mg"])
        st("scalar", (lambda s: s.activation(out=mg[:, 0:513], in_=mg[:, 0:513], func=AF.Ln)),
           reads=["mg"], writes=["mg"])
        st("scalar", (lambda s: s.activation(out=mg[:, 0:513], in_=mg[:, 0:513], func=AF.Exp, scale=0.5)),
           reads=["mg"], writes=["mg"])
        st("scalar", (lambda s: s.activation(out=mg[:, 0:513], in_=mg[:, 0:513], func=AF.Ln, bias=epsE[:, :])),
           reads=["mg", "epsE"], writes=["mg"])
        st("scalar", (lambda s: s.activation(out=inv_q[:, 0:513], in_=mg[:, 0:513], func=AF.Exp, scale=-1.0)),
           reads=["mg"], writes=["inv_q"])
        st("vector", (lambda v: (
            v.tensor_tensor(fqn[:, 0:512], fqraw[:, 0:512], inv_q[:, 0:512], op=ALU.mult),
            v.tensor_tensor(fqn[:, 512:1023], fqraw[:, 512:1023], inv_q[:, 0:511], op=ALU.mult),
            v.tensor_tensor(fqn[:, 1023:1024], fqraw[:, 1023:1024], inv_q[:, 512:513], op=ALU.mult))[-1]),
           reads=["fqraw", "inv_q"], writes=["fqn"])
        st("sync", (lambda s, i=i: s.dma_start(
            out=sp_fqn_d[i * TILE:(i + 1) * TILE, :], in_=fqn[:, :])),
           reads=["fqn"], writes=[f"spq{i}"], dma=True)
        st("scalar", (lambda s: (
            s.activation(out=mg[:, 0:512], in_=fkraw[:, 0:512], func=AF.Square),
            s.activation(out=mg2[:, 0:511], in_=fkraw[:, 512:1023], func=AF.Square),
            s.activation(out=mg[:, 512:513], in_=fkraw[:, 1023:1024], func=AF.Square))[-1]),
           reads=["fkraw", "mg", "mg2"], writes=["mg", "mg2"])
        st("vector", (lambda v: v.tensor_tensor(mg[:, 0:511], mg[:, 0:511], mg2[:, 0:511], op=ALU.add)),
           reads=["mg", "mg2"], writes=["mg"])
        st("scalar", (lambda s: s.activation(out=mg[:, 0:513], in_=mg[:, 0:513], func=AF.Ln)),
           reads=["mg"], writes=["mg"])
        st("scalar", (lambda s: s.activation(out=mg[:, 0:513], in_=mg[:, 0:513], func=AF.Exp, scale=0.5)),
           reads=["mg"], writes=["mg"])
        st("scalar", (lambda s: s.activation(out=mg[:, 0:513], in_=mg[:, 0:513], func=AF.Ln, bias=epsE[:, :])),
           reads=["mg", "epsE"], writes=["mg"])
        st("scalar", (lambda s: s.activation(out=inv_k[:, 0:513], in_=mg[:, 0:513], func=AF.Exp, scale=-1.0)),
           reads=["mg"], writes=["inv_k"])
        st("vector", (lambda v: (
            v.tensor_tensor(fkn[:, 0:512], fkraw[:, 0:512], inv_k[:, 0:512], op=ALU.mult),
            v.tensor_tensor(fkn[:, 512:1023], fkraw[:, 512:1023], inv_k[:, 0:511], op=ALU.mult),
            v.tensor_tensor(fkn[:, 1023:1024], fkraw[:, 1023:1024], inv_k[:, 512:513], op=ALU.mult))[-1]),
           reads=["fkraw", "inv_k"], writes=["fkn"])
        st("tensor", (lambda t, XN_T=XN_T: em_proj(t, XN_T, 2 * D, psA)),
           reads=[xnTn, "A_sb"], writes=["psA_h0", "psA_h1"])
        st("scalar", (lambda s: s.copy(out=fv[:, :], in_=psA[:, :])),
           reads=["psA_h0", "psA_h1"], writes=["fv"])
        st("vector", (lambda v: v.tensor_tensor(mg[:, 0:512], fkn[:, 0:512], fv[:, 0:512], op=ALU.mult)),
           reads=["fkn", "fv", "mg"], writes=["mg"])
        st("vector", (lambda v: v.tensor_tensor(mg2[:, 0:511], fkn[:, 512:1023], fv[:, 512:1023], op=ALU.mult)),
           reads=["fkn", "fv", "mg2"], writes=["mg2"])
        st("gpsimd", (lambda g: g.tensor_tensor(inv_q[:, 0:511], fkn[:, 0:511], fv[:, 512:1023], op=ALU.mult)),
           reads=["fkn", "fv", "inv_q"], writes=["inv_q"])
        st("gpsimd", (lambda g: (
            g.tensor_tensor(inv_k[:, 0:511], fkn[:, 512:1023], fv[:, 0:511], op=ALU.mult),
            g.memset(dumm[:, :], 0.0))[-1]),
           reads=["fkn", "fv", "inv_k"], writes=["inv_k"])
        st("vector", (lambda v: v.tensor_tensor(Pb[:, 0:511], mg[:, 0:511], mg2[:, 0:511], op=ALU.subtract)),
           reads=["mg", "mg2"], writes=[Pbn])
        st("vector", (lambda v: v.tensor_copy(Pb[:, 511:512], mg[:, 511:512])),
           reads=["mg"], writes=[Pbn])
        st("vector", (lambda v: v.tensor_tensor(Pb[:, 512:1023], inv_q[:, 0:511], inv_k[:, 0:511], op=ALU.add)),
           reads=["inv_q", "inv_k"], writes=[Pbn])
        st("vector", (lambda v: v.tensor_tensor(Pb[:, 1023:1024], fkn[:, 1023:1024], fv[:, 1023:1024], op=ALU.mult)),
           reads=["fkn", "fv"], writes=[Pbn])


    def a_tail(i):
        Pb = Pb_[i % 2]
        Pbn = f"Pb{i % 2}"
        def em_prefix(t):
            last = None
            for n in range(2):
                last = t.matmul(psC[:, n * 512:(n + 1) * 512], trilb[:, :],
                                Pb[:, n * 512:(n + 1) * 512], start=True, stop=True)
            return last

        st("tensor", em_prefix, reads=[Pbn, "trilb"], writes=["psC_h0", "psC_h1"])
        st("scalar", (lambda s: s.copy(out=fkvb[:, :], in_=psC[:, :])),
           reads=["psC_h0", "psC_h1"], writes=["fkvb"])
        st("gpsimd", (lambda g, i=i: g.dma_start(
            out=sp_fkv_d[i * TILE:(i + 1) * TILE, :], in_=fkvb[:, :])),
           reads=["fkvb"], writes=[f"spf{i}"], dma=True)
        if i == 0:
            st("vector", (lambda v: (
                v.tensor_copy(y1[:, 0:512], psC[:, 0:512]),
                v.tensor_copy(y1[:, 512:1024], psC[:, 512:1024]))[-1]),
               reads=["psC_h0", "psC_h1", "fkvb"], writes=["y1"])
        else:
            st("vector", (lambda v: (
                v.tensor_tensor(y1[:, 0:512], y1[:, 0:512], psC[:, 0:512], op=ALU.add),
                v.tensor_tensor(y1[:, 512:1024], y1[:, 512:1024], psC[:, 512:1024], op=ALU.add))[-1]),
               reads=["psC_h0", "psC_h1", "y1", "fkvb"], writes=["y1"])
        st("gpsimd", (lambda g, i=i: g.dma_start(out=sp_carry_d[i:i + 1, :], in_=y1[127:128, :])),
           reads=["y1"], writes=[f"spc{i}"], dma=True)

    st("scalar", (lambda s: _load_act_set(s, nc, "natural_log_exp_and_others")), writes=["acttab"])
    a_stage1(0)
    a_stage1b(0)
    a_stage1(1)
    a_stage1b(1)
    for i in range(NT):
        if i + 2 < NT:
            a_stage1(i + 2)
        a_stage2(i)
        if i > 0:
            a_tail(i - 1)
        if i + 2 < NT:
            a_stage1b(i + 2)
    a_tail(NT - 1)

    with nc.Block() as blkA:
        sch.replay(blkA, sems, final_wait=False)
    sch = Sched(sch.count, sch.bufs)
    st = sch.step

    _flush = {s: v for s, v in sch.count.items()
              if not s.endswith("_dmai") and v > 0}
    st("gpsimd", (lambda g: g.nop()), extra_waits=_flush)
    st("gpsimd", (lambda g: g.dma_start(out=cc_in.ap()[:, :], in_=y1[127:128, :])),
       reads=["y1"], writes=["cc_in"], dma=True)
    st("gpsimd", (lambda g: g.collective_compute(
        "AllGather", ALU.bypass, replica_groups=[list(range(8))],
        ins=[cc_in.ap().opt()], outs=[cc_out.ap().opt()])),
       reads=["cc_in"], writes=["cc_out"])
    st("gpsimd", (lambda g: g.dma_start(out=ag_sb[:, :], in_=cc_out.ap()[:, :])),
       reads=["cc_out"], writes=["ag_sb"], dma=True)

    def em_selmm(t):
        last = None
        for n in range(2):
            last = t.matmul(psC[0:1, n * 512:(n + 1) * 512], sel_sb[:, :],
                            ag_sb[:, n * 512:(n + 1) * 512], start=True, stop=True)
        return last

    st("tensor", em_selmm, reads=["sel_sb", "ag_sb"], writes=["psC_h0", "psC_h1"])
    st("vector", (lambda v: v.tensor_copy(carry_in_b[:, :], psC[0:1, :])),
       reads=["psC_h0", "psC_h1"], writes=["carry_in_b"])

    with nc.Block() as blkC:
        sch.replay(blkC, sems, final_wait=False)
    sch = Sched(sch.count, sch.bufs)
    st = sch.step

    st("scalar", (lambda s: _load_act_set(s, nc, "sigmoid_and_others")), writes=["acttab"])
    fkvB = [fkvb, fkvb2]
    fqnB = [fqn, fqn2]
    for i in range(NT):
        p = i % 2
        XT, XN_T = xt[p], xnT[p]
        xtn, xnTn = f"xt{p}", f"xnT{p}"
        FKV, FQN = fkvB[p], fqnB[p]
        fkvn, fqnn = f"fkvb{p}", f"fqn{p}"
        NBI = lnstats[:, 3 * i + 1:3 * i + 2]
        RP1 = lnstats[:, 3 * i + 2:3 * i + 3]

        st("sync", (lambda s, i=i, XT=XT: s.dma_start(
            out=XT[:, :], in_=x_d[i * TILE:(i + 1) * TILE, :])),
           writes=[xtn], dma=True)
        st("sync", (lambda s, i=i, XN_T=XN_T: s.dma_start(
            out=XN_T[:, :], in_=sp_xnT_d[i * TILE:(i + 1) * TILE, :])),
           reads=[f"spx{i}"], writes=[xnTn], dma=True)
        st("sync", (lambda s, i=i, FKV=FKV: s.dma_start(
            out=FKV[:, :], in_=sp_fkv_d[i * TILE:(i + 1) * TILE, :])),
           reads=[f"spf{i}"], writes=[fkvn], dma=True)
        st("sync", (lambda s, i=i, FQN=FQN: s.dma_start(
            out=FQN[:, :], in_=sp_fqn_d[i * TILE:(i + 1) * TILE, :])),
           reads=[f"spq{i}"], writes=[fqnn], dma=True)
        if i > 0:
            st("gpsimd", (lambda g, i=i, SNP=snap_b[i % 2]: g.dma_start(
                out=SNP[:, :], in_=sp_carry_d[i - 1:i, :])),
               reads=[f"spc{i - 1}"], writes=[f"snap_b{i % 2}"], dma=True)

        st("scalar", (lambda s, XT=XT, RP1=RP1, NBI=NBI: s.activation(
            out=y1[:, :], in_=XT[:, :], func=AF.Identity, bias=NBI, scale=RP1)),
           reads=[xtn, f"nb{i}", f"rp1{i}"], writes=["y1"])

        st("vector", (lambda v, XN_T=XN_T: v.tensor_tensor(
            dm[:, :].rearrange("p (c k) -> p c k", k=TILE),
            xnT3(XN_T, 2, 130), xnT3(XN_T, 1, 129), op=ALU.subtract)),
           reads=[xnTn], writes=["dm"])
        st("vector", (lambda v: v.tensor_tensor(b1[:, :], dm[:, :], MUr[:, :], op=ALU.mult)),
           reads=["dm", "MUr"], writes=["b1"])
        st("vector", (lambda v, XN_T=XN_T: v.tensor_tensor(
            xr[:, :].rearrange("p (c k) -> p c k", k=TILE),
            b1[:, :].rearrange("p (c k) -> p c k", k=TILE), xnT3(XN_T, 1, 129), op=ALU.add)),
           reads=["b1", xnTn], writes=["xr"])
        st("gpsimd", (lambda g: g.tensor_tensor(b2[:, :], dm[:, :], MUk[:, :], op=ALU.mult)),
           reads=["dm", "MUk"], writes=["b2"])
        st("gpsimd", (lambda g, XN_T=XN_T: (
            g.tensor_tensor(
                xk[:, :].rearrange("p (c k) -> p c k", k=TILE),
                b2[:, :].rearrange("p (c k) -> p c k", k=TILE), xnT3(XN_T, 1, 129), op=ALU.add),
            g.memset(dumm[:, :], 0.0))[-1]),
           reads=["b2", xnTn], writes=["xk"])

        def em_remat(t, i=i, FKV=FKV, SNP=snap_b[i % 2]):
            last = None
            for n in range(2):
                t.matmul(psC[:, n * 512:(n + 1) * 512], idenb[:, :],
                         FKV[:, n * 512:(n + 1) * 512], start=True, stop=False)
                if i > 0:
                    t.matmul(psC[:, n * 512:(n + 1) * 512], ones_row[:, :],
                             SNP[:, n * 512:(n + 1) * 512], start=False, stop=False)
                last = t.matmul(psC[:, n * 512:(n + 1) * 512], ones_row[:, :],
                                carry_in_b[:, n * 512:(n + 1) * 512], start=False, stop=True)
            return last

        st("tensor", em_remat,
           reads=[fkvn, f"snap_b{i % 2}", "carry_in_b", "idenb", "ones_row"],
           writes=["psC_h0", "psC_h1"])
        st("scalar", (lambda s, FKV=FKV: s.copy(out=FKV[:, :], in_=psC[:, :])),
           reads=["psC_h0", "psC_h1"], writes=[fkvn])

        st("tensor", (lambda t: em_ffn_mm(t, xr, WrT, psB)),
           reads=["xr", "WrT"], writes=["psB_h0", "psB_h1"])
        st("scalar", (lambda s: s.activation(out=r_sb[:, :], in_=psB[:, :], func=AF.Sigmoid)),
           reads=["psB_h0", "psB_h1"], writes=["r_sb"])
        st("tensor", (lambda t: em_ffn_mm(t, xk, Wk2T, psA)),
           reads=["xk", "Wk2T", "fkvb"], writes=["psA_h0", "psA_h1"])
        st("scalar", (lambda s: s.activation(out=kk[:, :], in_=psA[:, :], func=AF.Relu)),
           reads=["psA_h0", "psA_h1"], writes=["kk"])

        st("vector", (lambda v, FKV=FKV, FQN=FQN: v.tensor_tensor(mg[:, 0:512], FKV[:, 0:512], FQN[:, 0:512], op=ALU.mult)),
           reads=[fkvn, fqnn, "mg"], writes=["mg"])
        st("vector", (lambda v, FKV=FKV, FQN=FQN: v.tensor_tensor(mg2[:, 0:511], FKV[:, 512:1023], FQN[:, 512:1023], op=ALU.mult)),
           reads=[fkvn, fqnn, "mg2"], writes=["mg2"])
        st("vector", (lambda v, FKV=FKV, FQN=FQN: v.tensor_tensor(inv_q[:, 0:511], FKV[:, 512:1023], FQN[:, 0:511], op=ALU.mult)),
           reads=[fkvn, fqnn, "inv_q"], writes=["inv_q"])
        st("vector", (lambda v, FKV=FKV, FQN=FQN: v.tensor_tensor(inv_k[:, 0:511], FKV[:, 0:511], FQN[:, 512:1023], op=ALU.mult)),
           reads=[fkvn, fqnn, "inv_k"], writes=["inv_k"])
        st("vector", (lambda v: v.tensor_tensor(Zb[:, 0:511], mg[:, 0:511], mg2[:, 0:511], op=ALU.add)),
           reads=["mg", "mg2"], writes=["Zb"])
        st("vector", (lambda v: v.tensor_copy(Zb[:, 511:512], mg[:, 511:512])),
           reads=["mg"], writes=["Zb"])
        st("vector", (lambda v: v.tensor_tensor(Zb[:, 512:1023], inv_q[:, 0:511], inv_k[:, 0:511], op=ALU.subtract)),
           reads=["inv_q", "inv_k"], writes=["Zb"])
        st("vector", (lambda v, FKV=FKV, FQN=FQN: v.tensor_tensor(Zb[:, 1023:1024], FKV[:, 1023:1024], FQN[:, 1023:1024], op=ALU.mult)),
           reads=[fkvn, fqnn], writes=["Zb"])

        st("tensor", (lambda t: [t.transpose(psT[:, k * TILE:(k + 1) * TILE],
                                             Zb[:, k * TILE:(k + 1) * TILE], idenb[:, :])
                                 for k in range(8)][-1]),
           reads=["Zb", "idenb"], writes=["psT"])
        st("scalar", (lambda s: s.copy(out=ZT[:, :], in_=psT[:, :])), reads=["psT"], writes=["ZT"])

        st("vector", (lambda v: v.tensor_tensor(Zb[:, :], kk[:, :], kk[:, :], op=ALU.mult)),
           reads=["kk", "Zb"], writes=["Zb"])
        st("tensor", (lambda t: [t.transpose(psT[:, k * TILE:(k + 1) * TILE],
                                             Zb[:, k * TILE:(k + 1) * TILE], idenb[:, :])
                                 for k in range(8)][-1]),
           reads=["Zb", "idenb", "ZT"], writes=["psT"])
        st("scalar", (lambda s: s.copy(out=kk[:, :], in_=psT[:, :])), reads=["psT"], writes=["kk"])

        def em_irfft(t):
            last = None
            for n in range(2):
                for kc in range(8):
                    last = t.matmul(
                        psB[:, n * 512:(n + 1) * 512],
                        ZT[:, kc * TILE:(kc + 1) * TILE],
                        C_sb[:, kc * D + n * 512:kc * D + (n + 1) * 512],
                        start=(kc == 0), stop=(kc == 7))
            return last

        st("tensor", em_irfft, reads=["ZT", "C_sb", "r_sb"], writes=["psB_h0", "psB_h1"])
        st("vector", (lambda v: v.tensor_tensor(y_sb[:, :], y1[:, :], psB[:, :], op=ALU.add)),
           reads=["y1", "psB_h0", "psB_h1", "y_stored"], writes=["y_sb"])

        st("tensor", (lambda t: em_ffn_mm(t, kk, Wv2T, psA)),
           reads=["kk", "Wv2T", "kk"], writes=["psA_h0", "psA_h1"])
        st("vector", (lambda v: v.tensor_tensor(g_s[:, :], r_sb[:, :], psA[:, :], op=ALU.mult)),
           reads=["r_sb", "psA_h0", "psA_h1", "g_s"], writes=["g_s"])
        st("vector", (lambda v: v.tensor_tensor(y_sb[:, :], y_sb[:, :], g_s[:, :], op=ALU.add)),
           reads=["y_sb", "g_s"], writes=["y_sb"])

        st("scalar", (lambda s, i=i: s.dma_start(out=y_d[i * TILE:(i + 1) * TILE, :], in_=y_sb[:, :])),
           reads=["y_sb"], writes=["y_stored"], dma=True)

    with nc.Block() as block:
        sch.replay(block, sems, final_wait=True)

    es.close()
    return nc


def kernel(L0_attn_x, L0_attn_Wq, L0_attn_Wk, L0_attn_Wv,
           L0_ffn_W_r, L0_ffn_W_k, L0_ffn_W_v, L0_ffn_mu_r, L0_ffn_mu_k):
    x = np.ascontiguousarray(np.asarray(L0_attn_x, dtype=np.float32))
    mpack, cpack, tri, iden = _host_constants()
    common = {
        "wq": np.ascontiguousarray(L0_attn_Wq, dtype=np.float32),
        "wk": np.ascontiguousarray(L0_attn_Wk, dtype=np.float32),
        "wv": np.ascontiguousarray(L0_attn_Wv, dtype=np.float32),
        "wrt": np.ascontiguousarray(np.asarray(L0_ffn_W_r, dtype=np.float32).T),
        "wk2t": np.ascontiguousarray(np.asarray(L0_ffn_W_k, dtype=np.float32).T),
        "wv2t": np.ascontiguousarray(np.asarray(L0_ffn_W_v, dtype=np.float32).T),
        "mur": np.ascontiguousarray(np.asarray(L0_ffn_mu_r, dtype=np.float32).reshape(8, TILE).T),
        "muk": np.ascontiguousarray(np.asarray(L0_ffn_mu_k, dtype=np.float32).reshape(8, TILE).T),
        "mpack": mpack,
        "cpack": cpack,
        "tri": tri,
        "iden": iden,
    }
    in_maps = []
    for c in range(NCORES):
        b, h = c // 2, c % 2
        im = dict(common)
        im["x"] = np.ascontiguousarray(x[b, h * TCORE:(h + 1) * TCORE])
        im["xprev"] = (np.zeros((1, D), dtype=np.float32) if h == 0
                       else np.ascontiguousarray(x[b, TCORE - 1:TCORE]))
        sel = np.zeros((8, 1), dtype=np.float32)
        if h == 1:
            sel[c - 1, 0] = 1.0
        im["sel"] = sel
        in_maps.append(im)
    nc = build_program()
    res = run_bass_kernel_spmd(nc, in_maps, core_ids=list(range(NCORES)))
    out = np.empty((BATCH, SEQ, D), dtype=np.float32)
    for c in range(NCORES):
        b, h = c // 2, c % 2
        out[b, h * TCORE:(h + 1) * TCORE] = res.results[c]["y"]
    return out


# version 34
# speedup vs baseline: 1.0380x; 1.0036x over previous
import numpy as np
import concourse.bass as bass
import concourse.mybir as mybir
from concourse.bass_utils import run_bass_kernel_spmd
from contextlib import ExitStack

F32 = mybir.dt.float32
BF16 = mybir.dt.bfloat16
AF = mybir.ActivationFunctionType
ALU = mybir.AluOpType

D = 1024
SEQ = 8192
BATCH = 4
NCORES = 8
TILE = 128
TCORE = SEQ // 2
NT = TCORE // TILE
EPS_LN = 1e-5
EPS_FFT = 1e-8

ENGINES = ("sync", "gpsimd", "scalar", "vector", "tensor")


def _host_constants():
    NF = 513
    f = np.arange(NF)
    e = np.arange(D)
    ang = 2.0 * np.pi * np.outer(e, f) / D
    Mre = np.cos(ang)
    Mim = -np.sin(ang)
    mpack = np.concatenate([Mre[:, 1:513], Mim[:, 1:512], Mre[:, 0:1]], axis=1)
    d_ = np.arange(D)
    ang2 = 2.0 * np.pi * np.outer(f, d_) / D
    cf = np.ones(NF) * 2.0
    cf[0] = 1.0
    cf[512] = 1.0
    Cre = (cf[:, None] * np.cos(ang2)) / D
    Cim = (-2.0 * np.sin(ang2[1:512])) / D
    cpack = np.concatenate([Cre[1:513], Cim, Cre[0:1]], axis=0)
    tri = np.triu(np.ones((TILE, TILE)))
    iden = np.eye(TILE)
    return (
        np.ascontiguousarray(mpack, dtype=np.float32),
        np.ascontiguousarray(cpack, dtype=np.float32),
        np.ascontiguousarray(tri, dtype=np.float32),
        np.ascontiguousarray(iden, dtype=np.float32),
    )


class Sched:

    DMA_POOL = 6

    def __init__(self, start=None, bufs=None):
        self.q = {e: [] for e in ENGINES}
        self.count = dict(start) if start else {}
        self.bufs = dict(bufs) if bufs else {}

    def _bump(self, sem, inc):
        v = self.count.get(sem, 0) + inc
        self.count[sem] = v
        return v

    def step(self, eng, emit, reads=(), writes=(), dma=False, extra_waits=None):
        waits = {}
        if extra_waits:
            waits.update(extra_waits)

        def dep(sem, val):
            waits[sem] = max(waits.get(sem, 0), val)

        for b in reads:
            rec = self.bufs.get(b)
            if rec and rec[0]:
                dep(*rec[0])
        for b in writes:
            rec = self.bufs.get(b)
            if rec:
                if rec[0]:
                    dep(*rec[0])
                for r in rec[1]:
                    dep(*r)
        if dma:
            idx = self.count.get(f"{eng}_dmai", 0)
            self.count[f"{eng}_dmai"] = idx + 1
            sem = f"{eng}_d{idx % self.DMA_POOL}"
            cur = self.count.get(sem, 0)
            if cur > 0:
                dep(sem, cur)
            val = self._bump(sem, 16)
        else:
            sem = eng
            val = self._bump(sem, 1)
        self.q[eng].append((emit, dict(waits), sem, 16 if dma else 1))
        for b in reads:
            self.bufs.setdefault(b, [None, []])[1].append((sem, val))
        for b in writes:
            self.bufs[b] = [(sem, val), []]
        return (sem, val)

    def replay(self, block, sems, final_wait=True):
        waited = {e: {} for e in ENGINES}

        def body_for(name):
            def body(engobj):
                w = waited[name]
                for emit, waits, sem, inc in self.q[name]:
                    for f, v in waits.items():
                        if w.get(f, 0) < v:
                            engobj.wait_ge(sems[f], v)
                            w[f] = v
                    last = emit(engobj)
                    last.then_inc(sems[sem], inc)
                if final_wait and name in ("sync", "gpsimd"):
                    for j in range(Sched.DMA_POOL):
                        s = f"{name}_d{j}"
                        v = self.count.get(s, 0)
                        if v > 0 and waited[name].get(s, 0) < v:
                            engobj.wait_ge(sems[s], v)

            return body

        for name in ENGINES:
            if self.q[name]:
                getattr(block, name)(body_for(name))


def _load_act_set(s, nc, set_name):
    from concourse.hw_specs import get_activation_tables
    idx = list(get_activation_tables(nc.m.arch)).index(set_name)
    return s.add_instruction(mybir.InstLoadActFuncSet(
        name=nc.get_next_instruction_name(), act_func_set_id=idx, ins=[], outs=[]))


def build_program():
    nc = bass.Bass("TRN2", target_bir_lowering=False)

    x_d = nc.dram_tensor("x", [TCORE, D], F32, kind="ExternalInput").ap()
    xprev_d = nc.dram_tensor("xprev", [1, D], F32, kind="ExternalInput").ap()
    sel_d = nc.dram_tensor("sel", [8, 1], F32, kind="ExternalInput").ap()
    wq_d = nc.dram_tensor("wq", [D, D], F32, kind="ExternalInput").ap()
    wk_d = nc.dram_tensor("wk", [D, D], F32, kind="ExternalInput").ap()
    wv_d = nc.dram_tensor("wv", [D, D], F32, kind="ExternalInput").ap()
    wrt_d = nc.dram_tensor("wrt", [D, D], F32, kind="ExternalInput").ap()
    wk2t_d = nc.dram_tensor("wk2t", [D, D], F32, kind="ExternalInput").ap()
    wv2t_d = nc.dram_tensor("wv2t", [D, D], F32, kind="ExternalInput").ap()
    mur_d = nc.dram_tensor("mur", [TILE, 8], F32, kind="ExternalInput").ap()
    muk_d = nc.dram_tensor("muk", [TILE, 8], F32, kind="ExternalInput").ap()
    mpack_d = nc.dram_tensor("mpack", [D, D], F32, kind="ExternalInput").ap()
    cpack_d = nc.dram_tensor("cpack", [D, D], F32, kind="ExternalInput").ap()
    tri_d = nc.dram_tensor("tri", [TILE, TILE], F32, kind="ExternalInput").ap()
    iden_d = nc.dram_tensor("iden", [TILE, TILE], F32, kind="ExternalInput").ap()
    y_d = nc.dram_tensor("y", [TCORE, D], F32, kind="ExternalOutput").ap()
    sp_xnT_d = nc.dram_tensor("sp_xnT", [NT * TILE, 8 * 130], BF16).ap()
    sp_fkv_d = nc.dram_tensor("sp_fkv", [NT * TILE, D], BF16).ap()
    sp_fqn_d = nc.dram_tensor("sp_fqn", [NT * TILE, D], BF16).ap()
    sp_carry_d = nc.dram_tensor("sp_carry", [NT, D], F32).ap()
    cc_in = nc.dram_tensor("cc_in", [1, D], F32)
    cc_out = nc.dram_tensor("cc_out", [8, D], F32)

    es = ExitStack()

    def S(name, shape, dt):
        return es.enter_context(nc.sbuf_tensor(name, shape, dt))

    def Ps(name, shape, dt):
        return es.enter_context(nc.psum_tensor(name, shape, dt))

    A_sb = S("A_sb", [TILE, 8 * 3 * D], BF16)
    C_sb = S("C_sb", [TILE, 8 * D], BF16)
    WrT = S("WrT", [TILE, 8 * D], BF16)
    Wk2T = S("Wk2T", [TILE, 8 * D], BF16)
    Wv2T = S("Wv2T", [TILE, 8 * D], BF16)
    trilb = S("trilb", [TILE, TILE], BF16)
    iden = S("iden_sb", [TILE, TILE], F32)
    idenb = S("idenb", [TILE, TILE], BF16)
    mur = S("mur_sb", [TILE, 8], F32)
    muk = S("muk_sb", [TILE, 8], F32)
    MUr = S("MUr", [TILE, 8 * TILE], BF16)
    MUk = S("MUk", [TILE, 8 * TILE], BF16)
    onesb = S("onesb", [TILE, TILE], BF16)
    ones_col = S("ones_col", [TILE, 1], BF16)
    ones_row = S("ones_row", [1, TILE], BF16)
    epsLN = S("epsLN", [TILE, 1], F32)
    epsE = S("epsE", [TILE, 1], F32)
    zerof = S("zerof", [TILE, 1], F32)
    dumm = S("dumm", [1, 1], F32)
    carry_in_b = S("carry_in_b", [1, D], BF16)
    xbound = S("xbound", [TILE, 8], BF16)
    lnstats = S("lnstats", [TILE, 3 * NT], F32)
    sel_sb = S("sel_sb", [8, 1], F32)
    ag_sb = S("ag_sb", [8, D], F32)

    psA = Ps("psA", [TILE, D], F32)
    psB = Ps("psB", [TILE, D], F32)
    psT = Ps("psT", [TILE, D], BF16)
    psC = Ps("psC", [TILE, D], F32)

    sems = {e: es.enter_context(nc.semaphore(f"sem_{e}")) for e in ENGINES}
    for qe in ("sync", "gpsimd", "scalar"):
        for j in range(Sched.DMA_POOL):
            sems[f"{qe}_d{j}"] = es.enter_context(nc.semaphore(f"sem_{qe}_d{j}"))

    es_pro = ExitStack()
    Mb = es_pro.enter_context(nc.sbuf_tensor("Mb", [TILE, 8 * D], BF16))
    Wb = es_pro.enter_context(nc.sbuf_tensor("Wb", [TILE, 8 * D], BF16))
    stg = [es_pro.enter_context(nc.sbuf_tensor(f"stg{j}", [TILE, D], F32)) for j in range(6)]
    pvrow = es_pro.enter_context(nc.sbuf_tensor("pvrow", [1, D], F32))
    pvn = es_pro.enter_context(nc.sbuf_tensor("pvn", [1, D], F32))
    pv_ss = es_pro.enter_context(nc.sbuf_tensor("pv_ss", [1, 2], F32))
    pv_st = es_pro.enter_context(nc.sbuf_tensor("pv_st", [1, 6], F32))

    sch = Sched()
    st = sch.step

    def em_memsets(g):
        g.memset(onesb[:, :], 1.0)
        g.memset(ones_col[:, :], 1.0)
        g.memset(ones_row[:, :], 1.0)
        g.memset(epsLN[:, :], EPS_LN)
        g.memset(epsE[:, :], EPS_FFT)
        return g.memset(zerof[:, :], 0.0)

    st("gpsimd", em_memsets,
       writes=["onesb", "ones_col", "ones_row", "epsLN", "epsE", "zerof"])

    st("sync", (lambda s: s.dma_start(out=iden[:, :], in_=iden_d[:, :])), writes=["iden"], dma=True)
    st("sync", (lambda s: s.dma_start(out=mur[:, :], in_=mur_d[:, :])), writes=["mur"], dma=True)
    st("sync", (lambda s: s.dma_start(out=muk[:, :], in_=muk_d[:, :])), writes=["muk"], dma=True)
    st("sync", (lambda s: s.dma_start(out=pvrow[:, :], in_=xprev_d[:, :])), writes=["pvrow"], dma=True)
    st("sync", (lambda s: s.dma_start(out=sel_sb[:, :], in_=sel_d[:, :])), writes=["sel_sb"], dma=True)

    _lc = [0]

    def load_cast(dst, dst_off, srcap, rows, dsttok):
        j = _lc[0] % 6
        _lc[0] += 1
        st("sync", (lambda s, j=j: s.dma_start(out=stg[j][:, :],
                                               in_=srcap[rows:rows + TILE, :])),
           writes=[f"stg{j}"], dma=True)
        eng = "vector" if (_lc[0] % 2 == 0) else "scalar"
        if eng == "vector":
            st("vector", (lambda v, j=j: v.tensor_copy(dst[:, dst_off:dst_off + D], stg[j][:, :])),
               reads=[f"stg{j}"], writes=[dsttok])
        else:
            st("scalar", (lambda s, j=j: s.copy(out=dst[:, dst_off:dst_off + D], in_=stg[j][:, :])),
               reads=[f"stg{j}"], writes=[dsttok])

    st("sync", (lambda s: s.dma_start(out=stg[0][:, 0:TILE], in_=tri_d[:, :])),
       writes=["stg0"], dma=True)
    st("vector", (lambda v: v.tensor_copy(trilb[:, :], stg[0][:, 0:TILE])),
       reads=["stg0"], writes=["trilb"])
    for k in range(8):
        load_cast(Mb, k * D, mpack_d, k * TILE, f"Mb{k}")
    st("scalar", (lambda s: s.copy(out=idenb[:, :], in_=iden[:, :])),
       reads=["iden"], writes=["idenb"])

    for nm, dst, src, srctok in (("MUr", MUr, mur, "mur"), ("MUk", MUk, muk, "muk")):
        for k in range(8):
            st("scalar",
               (lambda s, dst=dst, src=src, k=k: s.activation(
                   out=dst[:, k * TILE:(k + 1) * TILE], in_=onesb[:, :],
                   func=AF.Identity, bias=zerof[:, :], scale=src[:, k:k + 1])),
               reads=["onesb", srctok, "zerof"], writes=[nm])

    for wi, (wsrc, acol) in enumerate(((wq_d, 0), (wk_d, D), (wv_d, 2 * D))):
        for k in range(8):
            load_cast(Wb, k * D, wsrc, k * TILE, f"Wb{k}")
        for m in range(8):
            ps = psA if (m % 2 == 0) else psB
            pstok = "psA" if (m % 2 == 0) else "psB"
            for n in range(2):
                def em_prep(t, m=m, n=n, ps=ps):
                    last = None
                    for kc in range(8):
                        last = t.matmul(
                            ps[:, n * 512:(n + 1) * 512],
                            Wb[:, kc * D + m * TILE:kc * D + (m + 1) * TILE],
                            Mb[:, kc * D + n * 512:kc * D + (n + 1) * 512],
                            start=(kc == 0), stop=(kc == 7))
                    return last

                st("tensor", em_prep,
                   reads=[f"Wb{k}" for k in range(8)] + [f"Mb{k}" for k in range(8)],
                   writes=[f"{pstok}_h{n}"])
                col = m * 3 * D + acol + n * 512
                if n == 0:
                    st("vector",
                       (lambda v, ps=ps, col=col: v.tensor_copy(
                           A_sb[:, col:col + 512], ps[:, 0:512])),
                       reads=[f"{pstok}_h0"], writes=["A_sb"])
                else:
                    st("scalar",
                       (lambda s, ps=ps, col=col: s.copy(
                           out=A_sb[:, col:col + 512], in_=ps[:, 512:1024])),
                       reads=[f"{pstok}_h1"], writes=["A_sb"])

    for k in range(8):
        load_cast(C_sb, k * D, cpack_d, k * TILE, "C_sb")
    for name, dst, wsrc in (("WrT", WrT, wrt_d), ("Wk2T", Wk2T, wk2t_d), ("Wv2T", Wv2T, wv2t_d)):
        for k in range(8):
            load_cast(dst, k * D, wsrc, k * TILE, name)
    st("scalar", (lambda s: (
        s.activation(out=pvn[:, :], in_=pvrow[:, :], func=AF.Square, accum_out=pv_ss[:, 0:1]),
        s.activation(out=pvn[:, :], in_=pvrow[:, :], func=AF.Copy, accum_out=pv_ss[:, 1:2]),
        s.copy(out=dumm[:, :], in_=zerof[0:1, :]))[-1]),
       reads=["pvrow"], writes=["pv_ss", "pvn"])
    st("vector", (lambda v: v.tensor_scalar(pv_st[:, 0:1], pv_ss[:, 1:2], 1.0 / D, None, op0=ALU.mult)),
       reads=["pv_ss"], writes=["pv_m"])
    st("vector", (lambda v: v.tensor_scalar(pv_st[:, 1:2], pv_ss[:, 0:1], 1.0 / D, None, op0=ALU.mult)),
       reads=["pv_ss"], writes=["pv_x2"])
    st("vector", (lambda v: v.tensor_tensor(pv_st[:, 2:3], pv_st[:, 0:1], pv_st[:, 0:1], op=ALU.mult)),
       reads=["pv_m"], writes=["pv_m2"])
    st("vector", (lambda v: v.tensor_tensor(pv_st[:, 1:2], pv_st[:, 1:2], pv_st[:, 2:3], op=ALU.subtract)),
       reads=["pv_x2", "pv_m2"], writes=["pv_var"])
    st("scalar", (lambda s: s.activation(out=pv_st[:, 3:4], in_=pv_st[:, 1:2], func=AF.Sqrt,
                                         bias=epsLN[0:1, :])),
       reads=["pv_var", "epsLN"], writes=["pv_sd"])
    st("vector", (lambda v: v.reciprocal(pv_st[:, 3:4], pv_st[:, 3:4])),
       reads=["pv_sd"], writes=["pv_rs"])
    st("vector", (lambda v: v.tensor_scalar(pv_st[:, 4:5], pv_st[:, 0:1], pv_st[:, 3:4], -1.0,
                                            op0=ALU.mult, op1=ALU.mult)),
       reads=["pv_m", "pv_rs"], writes=["pv_nb"])
    st("scalar", (lambda s: s.activation(out=pvn[:, :], in_=pvrow[:, :], func=AF.Identity,
                                         bias=pv_st[0:1, 4:5], scale=pv_st[0:1, 3:4])),
       reads=["pvrow", "pv_nb", "pv_rs", "pvn"], writes=["pvn"])
    st("tensor", (lambda t: [t.transpose(psA[:, k:k + 1], pvn[0:1, k * TILE:(k + 1) * TILE],
                                         iden[0:1, 0:1]) for k in range(8)][-1]),
       reads=["pvn", "iden", "psA_h0", "psA_h1"], writes=["psA_h0"])
    st("vector", (lambda v: v.tensor_copy(xbound[:, :], psA[:, 0:8])),
       reads=["psA_h0"], writes=["xbound"])

    with nc.Block() as blk1:
        sch.replay(blk1, sems, final_wait=False)
    es_pro.close()
    _carry_bufs = sch.bufs

    xt = [S(f"xt{p}", [TILE, D], F32) for p in range(2)]
    ssum = S("ssum", [TILE, 1], F32)
    smean = S("smean", [TILE, 1], F32)
    mean = S("mean", [TILE, 1], F32)
    msq = S("msq", [TILE, 1], F32)
    var = S("var", [TILE, 1], F32)
    sd = S("sd", [TILE, 1], F32)
    xnb = S("xnb", [TILE, D], BF16)
    sq_scr = S("sq_scr", [TILE, D], BF16)
    g_s = S("g_s", [TILE, D], F32)
    xnT = [S(f"xnT{p}", [TILE, 8 * 130], BF16) for p in range(2)]
    mg = S("mg", [TILE, 516], F32)
    mg2 = S("mg2", [TILE, 512], F32)
    inv_q = S("inv_q", [TILE, 516], F32)
    inv_k = S("inv_k", [TILE, 516], F32)
    fqn = S("fqn", [TILE, D], BF16)
    fkn = S("fkn", [TILE, D], BF16)
    fv = S("fv", [TILE, D], BF16)
    Pb_ = [S(f"Pb{p}", [TILE, D], BF16) for p in range(2)]
    fkvb = S("fkvb", [TILE, D], BF16)
    fkvb2 = S("fkvb2", [TILE, D], BF16)
    fqn2 = S("fqn2", [TILE, D], BF16)
    fqraw = S("fqraw", [TILE, D], BF16)
    fkraw = S("fkraw", [TILE, D], BF16)
    Zb = S("Zb", [TILE, D], BF16)
    ZT = S("ZT", [TILE, D], BF16)
    dm = S("dm", [TILE, 8 * TILE], BF16)
    b1 = S("b1", [TILE, 8 * TILE], BF16)
    b2 = S("b2", [TILE, 8 * TILE], BF16)
    xr = S("xr", [TILE, 8 * TILE], BF16)
    xk = S("xk", [TILE, 8 * TILE], BF16)
    r_sb = S("r_sb", [TILE, D], F32)
    kk = S("kk", [TILE, D], BF16)
    y1 = S("y1", [TILE, D], F32)
    y_sb = S("y_sb", [TILE, D], F32)
    snap_b = [S(f"snap_b{p}", [1, D], BF16) for p in range(2)]

    sch = Sched(sch.count, _carry_bufs)
    st = sch.step

    def xnT3(buf, lo, hi):
        return buf[:, :].rearrange("p (c k) -> p c k", k=130)[:, :, lo:hi]

    def em_proj(t, XN_T, acol, ps):
        last = None
        for n in range(2):
            for kc in range(8):
                last = t.matmul(
                    ps[:, n * 512:(n + 1) * 512],
                    XN_T[:, kc * 130 + 2:kc * 130 + 130],
                    A_sb[:, kc * 3 * D + acol + n * 512:kc * 3 * D + acol + (n + 1) * 512],
                    start=(kc == 0), stop=(kc == 7))
        return last

    def em_ffn_mm(t, inb, wsb, ps):
        last = None
        for n in range(2):
            for kc in range(8):
                last = t.matmul(
                    ps[:, n * 512:(n + 1) * 512],
                    inb[:, kc * TILE:(kc + 1) * TILE],
                    wsb[:, kc * D + n * 512:kc * D + (n + 1) * 512],
                    start=(kc == 0), stop=(kc == 7))
        return last

    def a_stage1(i):
        p = i % 2
        XT, XN_T = xt[p], xnT[p]
        xtn, xnTn = f"xt{p}", f"xnT{p}"
        st("sync", (lambda s, i=i, XT=XT: s.dma_start(
            out=XT[:, :], in_=x_d[i * TILE:(i + 1) * TILE, :])),
           writes=[xtn], dma=True)
        RST = lnstats[:, 3 * i:3 * i + 1]
        NBI = lnstats[:, 3 * i + 1:3 * i + 2]
        RP1 = lnstats[:, 3 * i + 2:3 * i + 3]
        st("scalar", (lambda s, XT=XT: (
            s.activation(out=sq_scr[:, :], in_=XT[:, :], func=AF.Square, accum_out=ssum[:, :]),
            s.activation(out=sq_scr[:, :], in_=XT[:, :], func=AF.Copy, accum_out=smean[:, :]),
            s.copy(out=dumm[:, :], in_=zerof[0:1, :]))[-1]),
           reads=[xtn], writes=["ssum", "smean", "sq_scr"])
        st("vector", (lambda v: v.tensor_scalar(msq[:, :], smean[:, :], smean[:, :], 1.0 / (D * D),
                                                op0=ALU.mult, op1=ALU.mult)),
           reads=["smean"], writes=["msq"])
        st("vector", (lambda v: v.scalar_tensor_tensor(var[:, :], ssum[:, :], 1.0 / D, msq[:, :],
                                                       op0=ALU.mult, op1=ALU.subtract)),
           reads=["ssum", "msq"], writes=["var"])
        st("scalar", (lambda s: s.activation(out=sd[:, :], in_=var[:, :], func=AF.Ln, bias=epsLN[:, :])),
           reads=["var", "epsLN"], writes=["sd"])
        st("scalar", (lambda s: s.activation(out=sd[:, :], in_=sd[:, :], func=AF.Exp, scale=0.5)),
           reads=["sd"], writes=["sd"])
        st("vector", (lambda v, RST=RST: v.reciprocal(RST, sd[:, :])),
           reads=["sd"], writes=[f"rstd{i}"])
        st("vector", (lambda v, RST=RST, NBI=NBI: v.tensor_scalar(NBI, smean[:, :], RST, -1.0 / D,
                                                                  op0=ALU.mult, op1=ALU.mult)),
           reads=["smean", f"rstd{i}"], writes=[f"nb{i}"])
        st("vector", (lambda v, RST=RST, RP1=RP1: v.tensor_scalar(RP1, RST, 1.0, None, op0=ALU.add)),
           reads=[f"rstd{i}"], writes=[f"rp1{i}"])
        st("scalar", (lambda s, XT=XT, RST=RST, NBI=NBI: s.activation(
            out=xnb[:, :], in_=XT[:, :], func=AF.Identity, bias=NBI, scale=RST)),
           reads=[xtn, f"nb{i}", f"rstd{i}"], writes=["xnb"])
    def a_stage1b(i):
        p = i % 2
        XN_T = xnT[p]
        xnTn = f"xnT{p}"
        st("tensor", (lambda t: [t.transpose(psT[:, k * TILE:(k + 1) * TILE],
                                             xnb[:, k * TILE:(k + 1) * TILE], idenb[:, :])
                                 for k in range(8)][-1]),
           reads=["xnb", "idenb"], writes=["psT"])
        if i == 0:
            st("vector", (lambda v, XN_T=XN_T: v.tensor_copy(
                xnT3(XN_T, 1, 2), xbound[:, :].rearrange("p (c k) -> p c k", k=1)[:, :, 0:1])),
               reads=["xbound"], writes=[xnTn])
        else:
            o = xnT[1 - p]
            st("vector", (lambda v, XN_T=XN_T, o=o: v.tensor_copy(
                xnT3(XN_T, 1, 2), xnT3(o, 129, 130))),
               reads=[f"xnT{1 - p}"], writes=[xnTn])
        st("scalar", (lambda s, XN_T=XN_T: s.copy(
            out=xnT3(XN_T, 2, 130),
            in_=psT[:, :].rearrange("p (c k) -> p c k", k=TILE))),
           reads=["psT"], writes=[xnTn])
        st("sync", (lambda s, i=i, XN_T=XN_T: s.dma_start(
            out=sp_xnT_d[i * TILE:(i + 1) * TILE, :], in_=XN_T[:, :])),
           reads=[xnTn], writes=[f"spx{i}"], dma=True)

    def a_stage2(i):
        p = i % 2
        XN_T = xnT[p]
        xnTn = f"xnT{p}"
        Pb = Pb_[p]
        Pbn = f"Pb{p}"
        st("tensor", (lambda t, XN_T=XN_T: em_proj(t, XN_T, 0, psA)),
           reads=[xnTn, "A_sb"], writes=["psA_h0", "psA_h1"])
        st("vector", (lambda v: v.tensor_copy(fqraw[:, :], psA[:, :])),
           reads=["psA_h0", "psA_h1"], writes=["fqraw"])
        st("tensor", (lambda t, XN_T=XN_T: em_proj(t, XN_T, D, psB)),
           reads=[xnTn, "A_sb"], writes=["psB_h0", "psB_h1"])
        st("vector", (lambda v: v.tensor_copy(fkraw[:, :], psB[:, :])),
           reads=["psB_h0", "psB_h1"], writes=["fkraw"])
        st("scalar", (lambda s: (
            s.activation(out=mg[:, 0:512], in_=fqraw[:, 0:512], func=AF.Square),
            s.activation(out=mg2[:, 0:511], in_=fqraw[:, 512:1023], func=AF.Square),
            s.activation(out=mg[:, 512:513], in_=fqraw[:, 1023:1024], func=AF.Square))[-1]),
           reads=["fqraw"], writes=["mg", "mg2"])
        st("vector", (lambda v: v.tensor_tensor(mg[:, 0:511], mg[:, 0:511], mg2[:, 0:511], op=ALU.add)),
           reads=["mg", "mg2"], writes=["mg"])
        st("scalar", (lambda s: s.activation(out=mg[:, 0:513], in_=mg[:, 0:513], func=AF.Ln)),
           reads=["mg"], writes=["mg"])
        st("scalar", (lambda s: s.activation(out=mg[:, 0:513], in_=mg[:, 0:513], func=AF.Exp, scale=0.5)),
           reads=["mg"], writes=["mg"])
        st("scalar", (lambda s: s.activation(out=mg[:, 0:513], in_=mg[:, 0:513], func=AF.Ln, bias=epsE[:, :])),
           reads=["mg", "epsE"], writes=["mg"])
        st("scalar", (lambda s: s.activation(out=inv_q[:, 0:513], in_=mg[:, 0:513], func=AF.Exp, scale=-1.0)),
           reads=["mg"], writes=["inv_q"])
        st("vector", (lambda v: (
            v.tensor_tensor(fqn[:, 0:512], fqraw[:, 0:512], inv_q[:, 0:512], op=ALU.mult),
            v.tensor_tensor(fqn[:, 512:1023], fqraw[:, 512:1023], inv_q[:, 0:511], op=ALU.mult),
            v.tensor_tensor(fqn[:, 1023:1024], fqraw[:, 1023:1024], inv_q[:, 512:513], op=ALU.mult))[-1]),
           reads=["fqraw", "inv_q"], writes=["fqn"])
        st("sync", (lambda s, i=i: s.dma_start(
            out=sp_fqn_d[i * TILE:(i + 1) * TILE, :], in_=fqn[:, :])),
           reads=["fqn"], writes=[f"spq{i}"], dma=True)
        st("scalar", (lambda s: (
            s.activation(out=mg[:, 0:512], in_=fkraw[:, 0:512], func=AF.Square),
            s.activation(out=mg2[:, 0:511], in_=fkraw[:, 512:1023], func=AF.Square),
            s.activation(out=mg[:, 512:513], in_=fkraw[:, 1023:1024], func=AF.Square))[-1]),
           reads=["fkraw", "mg", "mg2"], writes=["mg", "mg2"])
        st("vector", (lambda v: v.tensor_tensor(mg[:, 0:511], mg[:, 0:511], mg2[:, 0:511], op=ALU.add)),
           reads=["mg", "mg2"], writes=["mg"])
        st("scalar", (lambda s: s.activation(out=mg[:, 0:513], in_=mg[:, 0:513], func=AF.Ln)),
           reads=["mg"], writes=["mg"])
        st("scalar", (lambda s: s.activation(out=mg[:, 0:513], in_=mg[:, 0:513], func=AF.Exp, scale=0.5)),
           reads=["mg"], writes=["mg"])
        st("scalar", (lambda s: s.activation(out=mg[:, 0:513], in_=mg[:, 0:513], func=AF.Ln, bias=epsE[:, :])),
           reads=["mg", "epsE"], writes=["mg"])
        st("scalar", (lambda s: s.activation(out=inv_k[:, 0:513], in_=mg[:, 0:513], func=AF.Exp, scale=-1.0)),
           reads=["mg"], writes=["inv_k"])
        st("vector", (lambda v: (
            v.tensor_tensor(fkn[:, 0:512], fkraw[:, 0:512], inv_k[:, 0:512], op=ALU.mult),
            v.tensor_tensor(fkn[:, 512:1023], fkraw[:, 512:1023], inv_k[:, 0:511], op=ALU.mult),
            v.tensor_tensor(fkn[:, 1023:1024], fkraw[:, 1023:1024], inv_k[:, 512:513], op=ALU.mult))[-1]),
           reads=["fkraw", "inv_k"], writes=["fkn"])
        st("tensor", (lambda t, XN_T=XN_T: em_proj(t, XN_T, 2 * D, psC)),
           reads=[xnTn, "A_sb"], writes=["psC_h0", "psC_h1"])
        st("scalar", (lambda s: s.copy(out=fv[:, :], in_=psC[:, :])),
           reads=["psC_h0", "psC_h1"], writes=["fv"])
        st("vector", (lambda v: v.tensor_tensor(mg[:, 0:512], fkn[:, 0:512], fv[:, 0:512], op=ALU.mult)),
           reads=["fkn", "fv", "mg"], writes=["mg"])
        st("vector", (lambda v: v.tensor_tensor(mg2[:, 0:511], fkn[:, 512:1023], fv[:, 512:1023], op=ALU.mult)),
           reads=["fkn", "fv", "mg2"], writes=["mg2"])
        st("gpsimd", (lambda g: g.tensor_tensor(inv_q[:, 0:511], fkn[:, 0:511], fv[:, 512:1023], op=ALU.mult)),
           reads=["fkn", "fv", "inv_q"], writes=["inv_q"])
        st("gpsimd", (lambda g: (
            g.tensor_tensor(inv_k[:, 0:511], fkn[:, 512:1023], fv[:, 0:511], op=ALU.mult),
            g.memset(dumm[:, :], 0.0))[-1]),
           reads=["fkn", "fv", "inv_k"], writes=["inv_k"])
        st("vector", (lambda v: v.tensor_tensor(Pb[:, 0:511], mg[:, 0:511], mg2[:, 0:511], op=ALU.subtract)),
           reads=["mg", "mg2"], writes=[Pbn])
        st("vector", (lambda v: v.tensor_copy(Pb[:, 511:512], mg[:, 511:512])),
           reads=["mg"], writes=[Pbn])
        st("vector", (lambda v: v.tensor_tensor(Pb[:, 512:1023], inv_q[:, 0:511], inv_k[:, 0:511], op=ALU.add)),
           reads=["inv_q", "inv_k"], writes=[Pbn])
        st("vector", (lambda v: v.tensor_tensor(Pb[:, 1023:1024], fkn[:, 1023:1024], fv[:, 1023:1024], op=ALU.mult)),
           reads=["fkn", "fv"], writes=[Pbn])


    def a_tail(i):
        Pb = Pb_[i % 2]
        Pbn = f"Pb{i % 2}"
        def em_prefix(t):
            last = None
            for n in range(2):
                last = t.matmul(psC[:, n * 512:(n + 1) * 512], trilb[:, :],
                                Pb[:, n * 512:(n + 1) * 512], start=True, stop=True)
            return last

        st("tensor", em_prefix, reads=[Pbn, "trilb"], writes=["psC_h0", "psC_h1"])
        st("scalar", (lambda s: s.copy(out=fkvb[:, :], in_=psC[:, :])),
           reads=["psC_h0", "psC_h1"], writes=["fkvb"])
        st("gpsimd", (lambda g, i=i: g.dma_start(
            out=sp_fkv_d[i * TILE:(i + 1) * TILE, :], in_=fkvb[:, :])),
           reads=["fkvb"], writes=[f"spf{i}"], dma=True)
        if i == 0:
            st("vector", (lambda v: (
                v.tensor_copy(y1[:, 0:512], psC[:, 0:512]),
                v.tensor_copy(y1[:, 512:1024], psC[:, 512:1024]))[-1]),
               reads=["psC_h0", "psC_h1", "fkvb"], writes=["y1"])
        else:
            st("vector", (lambda v: (
                v.tensor_tensor(y1[:, 0:512], y1[:, 0:512], psC[:, 0:512], op=ALU.add),
                v.tensor_tensor(y1[:, 512:1024], y1[:, 512:1024], psC[:, 512:1024], op=ALU.add))[-1]),
               reads=["psC_h0", "psC_h1", "y1", "fkvb"], writes=["y1"])
        st("gpsimd", (lambda g, i=i: g.dma_start(out=sp_carry_d[i:i + 1, :], in_=y1[127:128, :])),
           reads=["y1"], writes=[f"spc{i}"], dma=True)

    st("scalar", (lambda s: _load_act_set(s, nc, "natural_log_exp_and_others")), writes=["acttab"])
    a_stage1(0)
    a_stage1b(0)
    a_stage1(1)
    a_stage1b(1)
    for i in range(NT):
        if i + 2 < NT:
            a_stage1(i + 2)
        a_stage2(i)
        if i > 0:
            a_tail(i - 1)
        if i + 2 < NT:
            a_stage1b(i + 2)
    a_tail(NT - 1)

    with nc.Block() as blkA:
        sch.replay(blkA, sems, final_wait=False)
    sch = Sched(sch.count, sch.bufs)
    st = sch.step

    _flush = {s: v for s, v in sch.count.items()
              if not s.endswith("_dmai") and v > 0}
    st("gpsimd", (lambda g: g.nop()), extra_waits=_flush)
    st("gpsimd", (lambda g: g.dma_start(out=cc_in.ap()[:, :], in_=y1[127:128, :])),
       reads=["y1"], writes=["cc_in"], dma=True)
    st("gpsimd", (lambda g: g.collective_compute(
        "AllGather", ALU.bypass, replica_groups=[list(range(8))],
        ins=[cc_in.ap().opt()], outs=[cc_out.ap().opt()])),
       reads=["cc_in"], writes=["cc_out"])
    st("gpsimd", (lambda g: g.dma_start(out=ag_sb[:, :], in_=cc_out.ap()[:, :])),
       reads=["cc_out"], writes=["ag_sb"], dma=True)

    def em_selmm(t):
        last = None
        for n in range(2):
            last = t.matmul(psC[0:1, n * 512:(n + 1) * 512], sel_sb[:, :],
                            ag_sb[:, n * 512:(n + 1) * 512], start=True, stop=True)
        return last

    st("tensor", em_selmm, reads=["sel_sb", "ag_sb"], writes=["psC_h0", "psC_h1"])
    st("vector", (lambda v: v.tensor_copy(carry_in_b[:, :], psC[0:1, :])),
       reads=["psC_h0", "psC_h1"], writes=["carry_in_b"])

    with nc.Block() as blkC:
        sch.replay(blkC, sems, final_wait=False)
    sch = Sched(sch.count, sch.bufs)
    st = sch.step

    st("scalar", (lambda s: _load_act_set(s, nc, "sigmoid_and_others")), writes=["acttab"])
    fkvB = [fkvb, fkvb2]
    fqnB = [fqn, fqn2]
    for i in range(NT):
        p = i % 2
        XT, XN_T = xt[p], xnT[p]
        xtn, xnTn = f"xt{p}", f"xnT{p}"
        FKV, FQN = fkvB[p], fqnB[p]
        fkvn, fqnn = f"fkvb{p}", f"fqn{p}"
        NBI = lnstats[:, 3 * i + 1:3 * i + 2]
        RP1 = lnstats[:, 3 * i + 2:3 * i + 3]

        st("sync", (lambda s, i=i, XT=XT: s.dma_start(
            out=XT[:, :], in_=x_d[i * TILE:(i + 1) * TILE, :])),
           writes=[xtn], dma=True)
        st("sync", (lambda s, i=i, XN_T=XN_T: s.dma_start(
            out=XN_T[:, :], in_=sp_xnT_d[i * TILE:(i + 1) * TILE, :])),
           reads=[f"spx{i}"], writes=[xnTn], dma=True)
        st("sync", (lambda s, i=i, FKV=FKV: s.dma_start(
            out=FKV[:, :], in_=sp_fkv_d[i * TILE:(i + 1) * TILE, :])),
           reads=[f"spf{i}"], writes=[fkvn], dma=True)
        st("sync", (lambda s, i=i, FQN=FQN: s.dma_start(
            out=FQN[:, :], in_=sp_fqn_d[i * TILE:(i + 1) * TILE, :])),
           reads=[f"spq{i}"], writes=[fqnn], dma=True)
        if i > 0:
            st("gpsimd", (lambda g, i=i, SNP=snap_b[i % 2]: g.dma_start(
                out=SNP[:, :], in_=sp_carry_d[i - 1:i, :])),
               reads=[f"spc{i - 1}"], writes=[f"snap_b{i % 2}"], dma=True)

        st("scalar", (lambda s, XT=XT, RP1=RP1, NBI=NBI: s.activation(
            out=y1[:, :], in_=XT[:, :], func=AF.Identity, bias=NBI, scale=RP1)),
           reads=[xtn, f"nb{i}", f"rp1{i}"], writes=["y1"])

        st("vector", (lambda v, XN_T=XN_T: v.tensor_tensor(
            dm[:, :].rearrange("p (c k) -> p c k", k=TILE),
            xnT3(XN_T, 2, 130), xnT3(XN_T, 1, 129), op=ALU.subtract)),
           reads=[xnTn], writes=["dm"])
        st("vector", (lambda v: v.tensor_tensor(b1[:, :], dm[:, :], MUr[:, :], op=ALU.mult)),
           reads=["dm", "MUr"], writes=["b1"])
        st("vector", (lambda v, XN_T=XN_T: v.tensor_tensor(
            xr[:, :].rearrange("p (c k) -> p c k", k=TILE),
            b1[:, :].rearrange("p (c k) -> p c k", k=TILE), xnT3(XN_T, 1, 129), op=ALU.add)),
           reads=["b1", xnTn], writes=["xr"])
        st("gpsimd", (lambda g: g.tensor_tensor(b2[:, :], dm[:, :], MUk[:, :], op=ALU.mult)),
           reads=["dm", "MUk"], writes=["b2"])
        st("gpsimd", (lambda g, XN_T=XN_T: (
            g.tensor_tensor(
                xk[:, :].rearrange("p (c k) -> p c k", k=TILE),
                b2[:, :].rearrange("p (c k) -> p c k", k=TILE), xnT3(XN_T, 1, 129), op=ALU.add),
            g.memset(dumm[:, :], 0.0))[-1]),
           reads=["b2", xnTn], writes=["xk"])

        def em_remat(t, i=i, FKV=FKV, SNP=snap_b[i % 2]):
            last = None
            for n in range(2):
                t.matmul(psC[:, n * 512:(n + 1) * 512], idenb[:, :],
                         FKV[:, n * 512:(n + 1) * 512], start=True, stop=False)
                if i > 0:
                    t.matmul(psC[:, n * 512:(n + 1) * 512], ones_row[:, :],
                             SNP[:, n * 512:(n + 1) * 512], start=False, stop=False)
                last = t.matmul(psC[:, n * 512:(n + 1) * 512], ones_row[:, :],
                                carry_in_b[:, n * 512:(n + 1) * 512], start=False, stop=True)
            return last

        st("tensor", em_remat,
           reads=[fkvn, f"snap_b{i % 2}", "carry_in_b", "idenb", "ones_row"],
           writes=["psC_h0", "psC_h1"])
        st("scalar", (lambda s, FKV=FKV: s.copy(out=FKV[:, :], in_=psC[:, :])),
           reads=["psC_h0", "psC_h1"], writes=[fkvn])

        st("tensor", (lambda t: em_ffn_mm(t, xr, WrT, psB)),
           reads=["xr", "WrT"], writes=["psB_h0", "psB_h1"])
        st("scalar", (lambda s: s.activation(out=r_sb[:, :], in_=psB[:, :], func=AF.Sigmoid)),
           reads=["psB_h0", "psB_h1"], writes=["r_sb"])
        st("tensor", (lambda t: em_ffn_mm(t, xk, Wk2T, psA)),
           reads=["xk", "Wk2T"], writes=["psA_h0", "psA_h1"])
        st("scalar", (lambda s: s.activation(out=kk[:, :], in_=psA[:, :], func=AF.Relu)),
           reads=["psA_h0", "psA_h1"], writes=["kk"])

        st("vector", (lambda v, FKV=FKV, FQN=FQN: v.tensor_tensor(mg[:, 0:512], FKV[:, 0:512], FQN[:, 0:512], op=ALU.mult)),
           reads=[fkvn, fqnn, "mg"], writes=["mg"])
        st("vector", (lambda v, FKV=FKV, FQN=FQN: v.tensor_tensor(mg2[:, 0:511], FKV[:, 512:1023], FQN[:, 512:1023], op=ALU.mult)),
           reads=[fkvn, fqnn, "mg2"], writes=["mg2"])
        st("vector", (lambda v, FKV=FKV, FQN=FQN: v.tensor_tensor(inv_q[:, 0:511], FKV[:, 512:1023], FQN[:, 0:511], op=ALU.mult)),
           reads=[fkvn, fqnn, "inv_q"], writes=["inv_q"])
        st("vector", (lambda v, FKV=FKV, FQN=FQN: v.tensor_tensor(inv_k[:, 0:511], FKV[:, 0:511], FQN[:, 512:1023], op=ALU.mult)),
           reads=[fkvn, fqnn, "inv_k"], writes=["inv_k"])
        st("vector", (lambda v: v.tensor_tensor(Zb[:, 0:511], mg[:, 0:511], mg2[:, 0:511], op=ALU.add)),
           reads=["mg", "mg2"], writes=["Zb"])
        st("vector", (lambda v: v.tensor_copy(Zb[:, 511:512], mg[:, 511:512])),
           reads=["mg"], writes=["Zb"])
        st("vector", (lambda v: v.tensor_tensor(Zb[:, 512:1023], inv_q[:, 0:511], inv_k[:, 0:511], op=ALU.subtract)),
           reads=["inv_q", "inv_k"], writes=["Zb"])
        st("vector", (lambda v, FKV=FKV, FQN=FQN: v.tensor_tensor(Zb[:, 1023:1024], FKV[:, 1023:1024], FQN[:, 1023:1024], op=ALU.mult)),
           reads=[fkvn, fqnn], writes=["Zb"])

        st("tensor", (lambda t: [t.transpose(psT[:, k * TILE:(k + 1) * TILE],
                                             Zb[:, k * TILE:(k + 1) * TILE], idenb[:, :])
                                 for k in range(8)][-1]),
           reads=["Zb", "idenb"], writes=["psT"])
        st("scalar", (lambda s: s.copy(out=ZT[:, :], in_=psT[:, :])), reads=["psT"], writes=["ZT"])

        st("vector", (lambda v: v.tensor_tensor(Zb[:, :], kk[:, :], kk[:, :], op=ALU.mult)),
           reads=["kk", "Zb"], writes=["Zb"])
        st("tensor", (lambda t: [t.transpose(psT[:, k * TILE:(k + 1) * TILE],
                                             Zb[:, k * TILE:(k + 1) * TILE], idenb[:, :])
                                 for k in range(8)][-1]),
           reads=["Zb", "idenb", "ZT"], writes=["psT"])
        st("scalar", (lambda s: s.copy(out=kk[:, :], in_=psT[:, :])), reads=["psT"], writes=["kk"])

        def em_irfft(t):
            last = None
            for n in range(2):
                for kc in range(8):
                    last = t.matmul(
                        psB[:, n * 512:(n + 1) * 512],
                        ZT[:, kc * TILE:(kc + 1) * TILE],
                        C_sb[:, kc * D + n * 512:kc * D + (n + 1) * 512],
                        start=(kc == 0), stop=(kc == 7))
            return last

        st("tensor", em_irfft, reads=["ZT", "C_sb", "r_sb"], writes=["psB_h0", "psB_h1"])
        st("vector", (lambda v: v.tensor_tensor(y_sb[:, :], y1[:, :], psB[:, :], op=ALU.add)),
           reads=["y1", "psB_h0", "psB_h1", "y_stored"], writes=["y_sb"])

        st("tensor", (lambda t: em_ffn_mm(t, kk, Wv2T, psA)),
           reads=["kk", "Wv2T", "kk"], writes=["psA_h0", "psA_h1"])
        st("vector", (lambda v: v.tensor_tensor(g_s[:, :], r_sb[:, :], psA[:, :], op=ALU.mult)),
           reads=["r_sb", "psA_h0", "psA_h1", "g_s"], writes=["g_s"])
        st("vector", (lambda v: v.tensor_tensor(y_sb[:, :], y_sb[:, :], g_s[:, :], op=ALU.add)),
           reads=["y_sb", "g_s"], writes=["y_sb"])

        st("scalar", (lambda s, i=i: s.dma_start(out=y_d[i * TILE:(i + 1) * TILE, :], in_=y_sb[:, :])),
           reads=["y_sb"], writes=["y_stored"], dma=True)

    with nc.Block() as block:
        sch.replay(block, sems, final_wait=True)

    es.close()
    return nc


def kernel(L0_attn_x, L0_attn_Wq, L0_attn_Wk, L0_attn_Wv,
           L0_ffn_W_r, L0_ffn_W_k, L0_ffn_W_v, L0_ffn_mu_r, L0_ffn_mu_k):
    x = np.ascontiguousarray(np.asarray(L0_attn_x, dtype=np.float32))
    mpack, cpack, tri, iden = _host_constants()
    common = {
        "wq": np.ascontiguousarray(L0_attn_Wq, dtype=np.float32),
        "wk": np.ascontiguousarray(L0_attn_Wk, dtype=np.float32),
        "wv": np.ascontiguousarray(L0_attn_Wv, dtype=np.float32),
        "wrt": np.ascontiguousarray(np.asarray(L0_ffn_W_r, dtype=np.float32).T),
        "wk2t": np.ascontiguousarray(np.asarray(L0_ffn_W_k, dtype=np.float32).T),
        "wv2t": np.ascontiguousarray(np.asarray(L0_ffn_W_v, dtype=np.float32).T),
        "mur": np.ascontiguousarray(np.asarray(L0_ffn_mu_r, dtype=np.float32).reshape(8, TILE).T),
        "muk": np.ascontiguousarray(np.asarray(L0_ffn_mu_k, dtype=np.float32).reshape(8, TILE).T),
        "mpack": mpack,
        "cpack": cpack,
        "tri": tri,
        "iden": iden,
    }
    in_maps = []
    for c in range(NCORES):
        b, h = c // 2, c % 2
        im = dict(common)
        im["x"] = np.ascontiguousarray(x[b, h * TCORE:(h + 1) * TCORE])
        im["xprev"] = (np.zeros((1, D), dtype=np.float32) if h == 0
                       else np.ascontiguousarray(x[b, TCORE - 1:TCORE]))
        sel = np.zeros((8, 1), dtype=np.float32)
        if h == 1:
            sel[c - 1, 0] = 1.0
        im["sel"] = sel
        in_maps.append(im)
    nc = build_program()
    res = run_bass_kernel_spmd(nc, in_maps, core_ids=list(range(NCORES)))
    out = np.empty((BATCH, SEQ, D), dtype=np.float32)
    for c in range(NCORES):
        b, h = c // 2, c % 2
        out[b, h * TCORE:(h + 1) * TCORE] = res.results[c]["y"]
    return out
